# version 14
# baseline (speedup 1.0000x reference)
import numpy as np
import concourse.bass as bass
import concourse.mybir as mybir
from concourse.bass_utils import run_bass_kernel_spmd

F32 = mybir.dt.float32
BF16 = mybir.dt.bfloat16
AF = mybir.ActivationFunctionType
ALU = mybir.AluOpType
AXX = mybir.AxisListType.X

NCORES = 8
D = 2048
T = 2048
TMC = 1024
NMC = T // TMC
DT = D // 128
NN = TMC // 512
HE = 1408
HT = HE // 128
NE = 8
HS = 2816
HST = HS // 128
FF = 6144
FT = FF // 128
FQ = 4
FQT = FT // FQ
EPS1 = 1e-5
EPS2 = 1e-8
CRG = 8.0

ENGS = ("sync", "scalar", "vector", "tensor", "gpsimd")

_DEBUG_TAPS = []

def set_debug_taps(taps):
    global _DEBUG_TAPS
    _DEBUG_TAPS = list(taps)



class Arena:

    def __init__(self, base, top):
        self.free_list = [(base, top - base)]
        self.live = {}
        self.dead = []

    def alloc(self, name, nbytes, align=32, high=False):
        nbytes = (nbytes + align - 1) & ~(align - 1)
        order = (range(len(self.free_list) - 1, -1, -1) if high
                 else range(len(self.free_list)))
        for i in order:
            off, size = self.free_list[i]
            if high:
                aoff = (off + size - nbytes) & ~(align - 1)
                if aoff < off:
                    continue
                newfree = []
                if aoff > off:
                    newfree.append((off, aoff - off))
                tail = (off + size) - (aoff + nbytes)
                if tail:
                    newfree.append((aoff + nbytes, tail))
            else:
                aoff = (off + align - 1) & ~(align - 1)
                pad = aoff - off
                if size - pad < nbytes:
                    continue
                rest = size - pad - nbytes
                newfree = []
                if pad:
                    newfree.append((off, pad))
                if rest:
                    newfree.append((aoff + nbytes, rest))
            self.free_list[i:i + 1] = newfree
            self.live[name] = (aoff, nbytes)
            hazards = []
            newdead = []
            for (o, s, n) in self.dead:
                if o + s <= aoff or o >= aoff + nbytes:
                    newdead.append((o, s, n))
                    continue
                hazards.append(n)
                if o < aoff:
                    newdead.append((o, aoff - o, n))
                if o + s > aoff + nbytes:
                    newdead.append((aoff + nbytes, o + s - (aoff + nbytes), n))
            self.dead = newdead
            return aoff, hazards
        raise MemoryError(
            f"SBUF arena: cannot alloc {name} ({nbytes}B); "
            f"free={self.free_list} live={ {k: v for k, v in self.live.items()} }")

    def free(self, name):
        off, size = self.live.pop(name)
        self.dead.append((off, size, name))
        self.free_list.append((off, size))
        self.free_list.sort()
        out = []
        for off, size in self.free_list:
            if out and out[-1][0] + out[-1][1] == off:
                out[-1] = (out[-1][0], out[-1][1] + size)
            else:
                out.append((off, size))
        self.free_list = out


class Buf:
    __slots__ = ("name", "h", "dtype", "nelem")

    def __init__(self, name, h, dtype, nelem):
        self.name = name
        self.h = h
        self.dtype = dtype
        self.nelem = nelem

    def ap(self):
        return self.h.ap()

    def __getitem__(self, sl):
        return self.h.ap()[sl]


class K:

    def __init__(self, nc):
        self.nc = nc
        self.ops = {e: [] for e in ENGS}
        self.cnt = {e: 0 for e in ENGS}
        self.dq = {"sync": 0, "gpsimd": 0, "scalar": 0}
        self.NDSLOT = {'gpsimd': 8, 'sync': 8, 'scalar': 8}
        self.dslot = {}
        self.last_w = {}
        self.readers = {}
        self.waited = {e: {} for e in ENGS}
        base = (nc.sbuf_base + 31) & ~31
        self.arena = Arena(base, nc.sbuf_top)
        self.bufs = {}
        self._uid = 0
        self.sems = {}

    def alloc(self, name, nelem, dtype, high=False):
        eb = 2 if dtype == BF16 else 4
        off, hazards = self.arena.alloc(name, nelem * eb, high=high)
        h = self.nc.alloc_sbuf_tensor_at(name + f"_{self._uid}", [128, nelem], dtype,
                                         offset=off)
        self._uid += 1
        b = Buf(name, h, dtype, nelem)
        self.bufs[name] = b
        toks = []
        for hn in hazards:
            if hn in self.last_w:
                toks.append(self.last_w[hn])
            toks.extend(self.readers.get(hn, ()))
        if toks:
            self.readers.setdefault(name, []).extend(toks)
        return b

    def free(self, name):
        pref = name + "."
        seen = set()
        for kk in list(self.last_w.keys()):
            if kk.startswith(pref):
                self.alias_key(name, kk)
                seen.add(kk)
        for kk in list(self.readers.keys()):
            if kk.startswith(pref) and kk not in seen:
                self.alias_key(name, kk)
        self.arena.free(name)
        del self.bufs[name]

    def alias_key(self, bufname, key):
        toks = list(self.readers.get(key, ()))
        t = self.last_w.get(key)
        if t:
            toks.append(t)
        self.readers.setdefault(bufname, []).extend(toks)

    def _dep_tokens(self, reads, writes):
        toks = []
        for k in reads:
            t = self.last_w.get(k)
            if t:
                toks.append(t)
        for k in writes:
            t = self.last_w.get(k)
            if t:
                toks.append(t)
            toks.extend(self.readers.get(k, ()))
        return toks

    def _commit(self, token, reads, writes):
        for k in writes:
            self.last_w[k] = token
            self.readers[k] = []
        for k in reads:
            self.readers.setdefault(k, []).append(token)

    def _emit_waits(self, engine, toks):
        need = {}
        for t in toks:
            kind, src, val = t
            if kind == "E" and src == engine:
                pass
            cur = need.get((kind, src), 0)
            if val > cur:
                need[(kind, src)] = val
        out = []
        for (kind, src), val in need.items():
            w = self.waited[engine]
            key = (kind, src)
            if w.get(key, 0) >= val:
                continue
            w[key] = val
            if kind == "E":
                out.append((src, val))
            else:
                out.append(("dma_" + src, 16 * val))
        return out

    def op(self, engine, fn, reads=(), writes=(), n_inc=1):
        toks = self._dep_tokens(reads, writes)
        waits = self._emit_waits(engine, toks)
        self.cnt[engine] += 1
        myval = self.cnt[engine]
        token = ("E", engine, myval)
        sem_of = self.sems

        def closure(eng, fn=fn, waits=waits, engine=engine, myval=myval):
            for sname, val in waits:
                eng.wait_ge(sem_of[sname], val)
            inst = fn(eng)
            inst.then_inc(sem_of[engine], 1)

        self.ops[engine].append(closure)
        self._commit(token, reads, writes)
        return token

    def dma(self, queue, out_ap, in_ap, reads=(), writes=(), **kw):
        toks = self._dep_tokens(reads, writes)
        slot = self.dq[queue] % self.NDSLOT[queue]
        self.dq[queue] += 1
        qs = f"{queue}:{slot}"
        prev = self.dslot.get(qs, 0)
        waits = self._emit_waits(queue, toks)
        self.dslot[qs] = prev + 1
        token = ("D", qs, self.dslot[qs])
        sem_of = self.sems

        def closure(eng, out_ap=out_ap, in_ap=in_ap, waits=waits, qs=qs, kw=kw):
            for sname, val in waits:
                eng.wait_ge(sem_of[sname], val)
            eng.dma_start(out=out_ap, in_=in_ap, **kw).then_inc(
                sem_of["dma_" + qs], 16)

        self.ops[queue].append(closure)
        self._commit(token, reads, writes)
        return token

    def full_barrier(self):
        toks = [("E", e, self.cnt[e]) for e in ENGS if self.cnt[e] > 0]
        toks += [("D", qs, n) for qs, n in self.dslot.items() if n > 0]
        for e in ENGS:
            self.barrier_wait(e, toks)

    def barrier_wait(self, engine, tokens):
        waits = self._emit_waits(engine, tokens)
        sem_of = self.sems

        def closure(eng, waits=waits):
            for sname, val in waits:
                eng.wait_ge(sem_of[sname], val)

        self.ops[engine].append(closure)

    def build_and_emit(self, es):
        nc = self.nc
        for e in ENGS:
            self.sems[e] = es.enter_context(nc.semaphore("sem_" + e))
        for q in self.dq:
            for s in range(self.NDSLOT[q]):
                self.sems[f"dma_{q}:{s}"] = es.enter_context(
                    nc.semaphore(f"dsem_{q}_{s}"))
        with nc.Block() as block:
            @block.sync
            def _(eng):
                for f in self.ops["sync"]:
                    f(eng)

            @block.scalar
            def _(eng):
                for f in self.ops["scalar"]:
                    f(eng)

            @block.vector
            def _(eng):
                for f in self.ops["vector"]:
                    f(eng)

            @block.tensor
            def _(eng):
                for f in self.ops["tensor"]:
                    f(eng)

            @block.gpsimd
            def _(eng):
                for f in self.ops["gpsimd"]:
                    f(eng)

    def v_tt(self, out, in0, in1, aluop, reads, writes):
        return self.op("vector", lambda e: e.tensor_tensor(out, in0, in1, aluop),
                       reads, writes)

    def v_stt(self, out, in0, scalar, in1, op0, op1, reads, writes):
        return self.op("vector",
                       lambda e: e.scalar_tensor_tensor(out, in0, scalar, in1, op0, op1),
                       reads, writes)

    def v_ts(self, out, in0, s1, s2, op0, op1, reads, writes):
        if s2 is None:
            return self.op("vector",
                           lambda e: e.tensor_scalar(out, in0, s1, None, op0),
                           reads, writes)
        return self.op("vector",
                       lambda e: e.tensor_scalar(out, in0, s1, s2, op0, op1),
                       reads, writes)

    def v_copy(self, out, in0, reads, writes):
        return self.op("vector", lambda e: e.tensor_copy(out, in0), reads, writes)

    def v_red(self, out, in0, aluop, reads, writes, negate=False):
        return self.op("vector",
                       lambda e: e.tensor_reduce(out, in0, AXX, aluop, negate=negate),
                       reads, writes)

    def v_recip(self, out, in0, reads, writes):
        return self.op("vector", lambda e: e.reciprocal(out, in0), reads, writes)

    def v_scan(self, out, a, b, init, reads, writes):
        return self.op("vector",
                       lambda e: e.tensor_tensor_scan(out, a, b, init, ALU.mult, ALU.add),
                       reads, writes)

    def v_memset(self, out, val, reads, writes):
        return self.op("vector", lambda e: e.memset(out, val), reads, writes)

    def s_act(self, out, in0, func, reads, writes, bias=0.0, scale=1.0):
        return self.op("scalar",
                       lambda e: e.activation(out, in0, func, bias=bias, scale=scale),
                       reads, writes)

    def g_tt(self, out, in0, in1, aluop, reads, writes):
        return self.op("gpsimd", lambda e: e.tensor_tensor(out, in0, in1, aluop),
                       reads, writes)

    def g_copy(self, out, in0, reads, writes):
        return self.op("gpsimd", lambda e: e.tensor_copy(out, in0), reads, writes)

    def mm_group(self, psum_ap, pairs, reads, writes, tag=None):
        if not hasattr(self, "mm_log"):
            self.mm_log = []
            self.mm_total = 0
        if tag is None:
            import traceback
            fr = traceback.extract_stack(limit=2)[0]
            tag = f"{fr.name}:{fr.lineno}"
        self.mm_total += len(pairs)
        self.mm_log.append((self.mm_total, tag))
        def fn(e):
            n = len(pairs)
            inst = None
            for i, (l, r) in enumerate(pairs):
                inst = e.matmul(psum_ap, l, r, start=(i == 0), stop=(i == n - 1))
            return inst
        return self.op("tensor", fn, reads, writes)

    def t_transpose(self, out_psum, in_sb, ident, reads, writes):
        if not hasattr(self, "mm_log"):
            self.mm_log = []
            self.mm_total = 0
        self.mm_total += 1
        self.mm_log.append((self.mm_total, "transpose"))
        return self.op("tensor", lambda e: e.transpose(out_psum, in_sb, ident),
                       reads, writes)



class Model:
    def __init__(self, nc, k: K):
        self.nc = nc
        self.k = k
        self.taps = []
        self.psi = 0

    def init_psum(self, es):
        self.pspool = []
        for i in range(4):
            h = es.enter_context(self.nc.psum_tensor(f"pp{i}", [128, 1024], F32))
            self.pspool.append((h, f"ps{i}"))

    def next_ps(self):
        h, key = self.pspool[self.psi % 4]
        self.psi += 1
        return h, key

    def tap(self, name, src_ap, nelem, dtype=F32):
        if name not in _DEBUG_TAPS:
            return
        h = self.nc.dram_tensor("tap_" + name, [128, nelem], dtype,
                                kind="ExternalOutput")
        keys = [f"tapkey_{name}"]
        self.taps.append(name)
        return h

    def tap_dma(self, name, src_ap, nelem, reads, dtype=F32):
        if name not in _DEBUG_TAPS:
            return
        h = self.nc.dram_tensor("tap_" + name, [128, nelem], dtype,
                                kind="ExternalOutput")
        self.k.dma("sync", h.ap(), src_ap, reads=reads, writes=())
        self.taps.append(name)

    def init_constants(self):
        k = self.k
        nc = self.nc
        self.ident = k.alloc("ident", 128, F32)
        self.oneh = k.alloc("oneh", NE * 128, F32)
        self.ones_col_bf = k.alloc("ones_col_bf", 1, BF16)
        self.ones_row = k.alloc("ones_row", 128, F32)

        k.op("gpsimd",
             lambda e: e.iota(self.ident[:, 0:128], pattern=[[-1, 128]], base=0,
                              channel_multiplier=1,
                              allow_small_or_imprecise_dtypes=True),
             reads=(), writes=("ident",))
        k.v_ts(self.ident[:, 0:128], self.ident[:, 0:128], 0.0, None,
               ALU.is_equal, None, reads=("ident",), writes=("ident",))
        k.op("gpsimd",
             lambda e: e.iota(self.oneh[0:NE, 0:NE * 128], pattern=[[-1, NE], [0, 128]],
                              base=0, channel_multiplier=1,
                              allow_small_or_imprecise_dtypes=True),
             reads=(), writes=("oneh",))
        k.v_ts(self.oneh[0:NE, 0:NE * 128], self.oneh[0:NE, 0:NE * 128], 0.0, None,
               ALU.is_equal, None, reads=("oneh",), writes=("oneh",))
        k.v_memset(self.ones_col_bf[:, 0:1], 1.0, reads=(), writes=("ones_col_bf",))
        k.v_memset(self.ones_row[0:1, 0:128], 1.0, reads=(), writes=("ones_row",))
        self.cb = k.alloc("cbias", 4, F32)
        for i, val in enumerate((EPS1, EPS2, 1.0)):
            k.v_memset(self.cb[:, i:i + 1], val, reads=(), writes=("cbias",))

    def cbias(self, idx):
        return self.cb[:, idx:idx + 1]

    def init_vectors(self, dr):
        k = self.k
        self.vec = {}
        specs = [
            ("norm1_w", DT), ("norm2_w", DT),
            ("left_b", DT), ("right_b", DT), ("out_b", DT), ("conv_b", DT),
            ("down_b", DT), ("up1_b", FT), ("up2_b", FT),
            ("ga_b", DT), ("gx_b", DT), ("a_param", DT),
            ("conv_w", DT * 4),
        ]
        for name, n in specs:
            b = k.alloc("v_" + name, n, F32)
            k.dma("sync", b[:, 0:n], dr[name].ap(), reads=(), writes=("v_" + name,))
            self.vec[name] = b
        gwf = k.alloc("v_gate_w_f", DT * NE, F32)
        k.dma("sync", gwf[:, 0:DT * NE], dr["gate_w"].ap(), reads=(),
              writes=("v_gate_w_f",))
        self.gwb = k.alloc("v_gate_w", DT * NE, BF16)
        k.v_copy(self.gwb[:, 0:DT * NE], gwf[:, 0:DT * NE],
                 reads=("v_gate_w_f",), writes=("v_gate_w",))
        for nm in ("ga_b", "gx_b"):
            b = k.alloc("v_" + nm + "_h", DT, F32)
            k.v_ts(b[:, 0:DT], self.vec[nm][:, 0:DT], 0.5, None, ALU.mult, None,
                   reads=("v_" + nm,), writes=("v_" + nm + "_h",))
            self.vec[nm + "_h"] = b
        ap_ = self.vec["a_param"]
        t1 = k.alloc("v_sp_t1", DT, F32)
        t2 = k.alloc("v_sp_t2", DT, F32)
        m4sp = k.alloc("v_m4sp", DT, F32)
        k.s_act(t1[:, 0:DT], ap_[:, 0:DT], AF.Abs, reads=("v_a_param",),
                writes=("v_sp_t1",))
        k.s_act(t1[:, 0:DT], t1[:, 0:DT], AF.Exp, reads=("v_sp_t1",),
                writes=("v_sp_t1",), scale=-1.0)
        k.s_act(t1[:, 0:DT], t1[:, 0:DT], AF.Ln, reads=("v_sp_t1", "cbias"),
                writes=("v_sp_t1",), bias=self.cb[:, 2:3])
        k.v_ts(t2[:, 0:DT], ap_[:, 0:DT], 0.0, None, ALU.max, None,
               reads=("v_a_param",), writes=("v_sp_t2",))
        k.v_tt(m4sp[:, 0:DT], t1[:, 0:DT], t2[:, 0:DT], ALU.add,
               reads=("v_sp_t1", "v_sp_t2"), writes=("v_m4sp",))
        k.v_ts(m4sp[:, 0:DT], m4sp[:, 0:DT], -4.0, None, ALU.mult, None,
               reads=("v_m4sp",), writes=("v_m4sp",))
        self.m4sp = m4sp

    def wblock_dma(self, dstbuf, dst_off, wdram, k0t, nk, m0, mb, wkey):
        src = wdram.ap().rearrange("(kt p) m -> p kt m", p=128)[
            :, k0t:k0t + nk, m0:m0 + mb]
        dst = dstbuf[:, dst_off:dst_off + nk * mb].rearrange(
            "p (kt m) -> p kt m", kt=nk)
        self.k.dma("gpsimd", dst, src, reads=(), writes=(wkey,))

    def stats_rows(self, xbuf, xkey, want_mean, tag, tile_keys=None):
        k = self.k
        xbf = k.alloc(f"st_xbf_{tag}", DT * TMC, BF16) if want_mean else None
        sq = k.alloc(f"st_sq_{tag}", DT * TMC, BF16)
        ssq_row = k.alloc(f"row_ssq_{tag}", TMC, F32)
        sum_row = k.alloc(f"row_sum_{tag}", TMC, F32) if want_mean else None
        onescol = self.ones_col_bf[:, 0:1]
        for kt in range(DT):
            sl = slice(kt * TMC, (kt + 1) * TMC)
            rk = (tile_keys[kt],) if tile_keys else (xkey,)
            if want_mean:
                k.v_copy(xbf[:, sl], xbuf[:, sl], reads=rk, writes=(xbf.name,))
                k.op("scalar",
                     lambda e, o=sq[:, sl], i=xbf[:, sl]:
                         e.activation(o, i, AF.Square),
                     reads=(xbf.name,), writes=(sq.name,))
            else:
                k.op("scalar",
                     lambda e, o=sq[:, sl], i=xbuf[:, sl]:
                         e.activation(o, i, AF.Square),
                     reads=rk, writes=(sq.name,))
        for n in range(NN):
            if want_mean:
                ps, pk = self.next_ps()
                pairs = [(onescol, xbf[:, kt * TMC + n * 512: kt * TMC + n * 512 + 512])
                         for kt in range(DT)]
                k.mm_group(ps.ap()[0:1, 0:512], pairs,
                           reads=(xbf.name, "ones_col_bf"), writes=(pk,))
                k.v_copy(sum_row[0:1, n * 512:(n + 1) * 512], ps.ap()[0:1, 0:512],
                         reads=(pk,), writes=(sum_row.name,))
            ps, pk = self.next_ps()
            pairs = [(onescol, sq[:, kt * TMC + n * 512: kt * TMC + n * 512 + 512])
                     for kt in range(DT)]
            k.mm_group(ps.ap()[0:1, 0:512], pairs, reads=(sq.name, "ones_col_bf"),
                       writes=(pk,))
            k.v_copy(ssq_row[0:1, n * 512:(n + 1) * 512], ps.ap()[0:1, 0:512],
                     reads=(pk,), writes=(ssq_row.name,))
        if xbf is not None:
            k.free(xbf.name)
        k.free(sq.name)
        return sum_row, ssq_row

    def rstd_row(self, out, outkey, var_ap, varkey, eps_idx, tag):
        k = self.k
        k.s_act(out, var_ap, AF.Ln, reads=(varkey, "cbias"), writes=(outkey,),
                bias=self.cb[0:1, eps_idx:eps_idx + 1])
        k.s_act(out, out, AF.Exp, reads=(outkey,), writes=(outkey,), scale=-0.5)

    def bcast_row(self, rowbuf, rowkey, tag):
        k = self.k
        B = k.alloc(f"bc_{tag}", TMC, F32)
        for n in range(NN):
            ps, pk = self.next_ps()
            k.mm_group(ps.ap()[:, 0:512],
                       [(self.ones_row[0:1, 0:128],
                         rowbuf[0:1, n * 512:(n + 1) * 512])],
                       reads=(rowkey, "ones_row"), writes=(pk,))
            k.v_copy(B[:, n * 512:(n + 1) * 512], ps.ap()[:, 0:512], reads=(pk,),
                     writes=(B.name,))
        return B

    def mm_stream(self, wdram, Mtiles, rhs_of, evac, tag, ring_slots=3,
                  rhs_keys=(), nk=DT, tpb=2):
        k = self.k
        nb = (Mtiles + tpb - 1) // tpb
        ring = []
        for i in range(ring_slots):
            bu = k.alloc(f"wr_{tag}_{i}", nk * 128 * tpb, BF16)
            ring.append(bu)

        def loadblk(bi):
            slot = ring[bi % ring_slots]
            m0 = bi * tpb * 128
            mb = min(tpb * 128, Mtiles * 128 - m0)
            self.wblock_dma(slot, 0, wdram, 0, nk, m0, mb, slot.name)

        for bi in range(min(2, nb)):
            loadblk(bi)
        for bi in range(nb):
            if bi + 2 < nb:
                loadblk(bi + 2)
            slot = ring[bi % ring_slots]
            mhere = min(tpb, Mtiles - bi * tpb)
            bw = mhere * 128
            for mi in range(mhere):
                mt = bi * tpb + mi
                ps, pk = self.next_ps()
                for n in range(NN):
                    pairs = [(slot[:, kt * bw + mi * 128: kt * bw + mi * 128 + 128],
                              rhs_of(kt, n)) for kt in range(nk)]
                    k.mm_group(ps.ap()[:, n * 512:(n + 1) * 512], pairs,
                               reads=(slot.name,) + tuple(rhs_keys), writes=(pk,))
                evac(mt, ps, pk)
        for bu in ring:
            k.free(bu.name)

    def ffn_stage(self, xb, xkey, mc, tag, tile_keys=None):
        k = self.k
        nc = self.nc
        sum_row, ssq_row = self.stats_rows(
            xb, xkey, True, tag,
            tile_keys=tile_keys if getattr(build_core, "TK_STATS", True) else None)
        m_row = k.alloc(f"row_m_{tag}", TMC, F32)
        v_row = k.alloc(f"row_v_{tag}", TMC, F32)
        s_row = k.alloc(f"row_s_{tag}", TMC, F32)
        rT = slice(0, TMC)
        k.v_ts(m_row[0:1, rT], sum_row[0:1, rT], 1.0 / D, None, ALU.mult, None,
               reads=(sum_row.name,), writes=(m_row.name,))
        k.op("vector", lambda e: e.tensor_tensor(v_row[0:1, rT], m_row[0:1, rT],
                                                 m_row[0:1, rT], ALU.mult),
             reads=(m_row.name,), writes=(v_row.name,))
        k.v_stt(v_row[0:1, rT], ssq_row[0:1, rT], 1.0 / D, v_row[0:1, rT],
                ALU.mult, ALU.subtract, reads=(ssq_row.name, v_row.name),
                writes=(v_row.name,))
        self.rstd_row(s_row[0:1, rT], s_row.name, v_row[0:1, rT], v_row.name,
                      0, tag)
        mB = self.bcast_row(m_row, m_row.name, f"m_{tag}")
        sB = self.bcast_row(s_row, s_row.name, f"s_{tag}")
        k.free(sum_row.name); k.free(ssq_row.name)
        k.free(m_row.name); k.free(v_row.name); k.free(s_row.name)
        u = k.alloc(f"u_{tag}", DT * TMC, BF16)
        tn = k.alloc(f"un_{tag}", TMC, F32)
        for kt in range(DT):
            sl = slice(kt * TMC, (kt + 1) * TMC)
            xrk = tile_keys[kt] if tile_keys else xkey
            k.v_tt(tn[:, 0:TMC], xb[:, sl], mB[:, 0:TMC], ALU.subtract,
                   reads=(xrk, mB.name), writes=(tn.name,))
            k.v_tt(u[:, sl], tn[:, 0:TMC], sB[:, 0:TMC], ALU.mult,
                   reads=(tn.name, sB.name), writes=(u.name,))
        k.free(tn.name)
        k.free(mB.name); k.free(sB.name)
        self.tap_dma(f"u_{tag}", u[:, 0:DT * TMC], DT * TMC, (u.name,), BF16)

        combT = k.alloc(f"combT_{tag}", TMC, F32)
        sm = k.alloc(f"sm_{tag}", 8, F32)
        sc = k.alloc(f"sc_{tag}", 8, F32)
        r1 = k.alloc(f"r1_{tag}", 8, F32)
        ct = k.alloc(f"ct_{tag}", 8, F32)
        red = k.alloc(f"red_{tag}", 4, F32)
        for tt in range(TMC // 128):
            ps, pk = self.next_ps()
            pairs = [(u[:, kt * TMC + tt * 128: kt * TMC + tt * 128 + 128],
                      self.gwb[:, kt * NE:(kt + 1) * NE]) for kt in range(DT)]
            k.mm_group(ps.ap()[:, 0:NE], pairs, reads=(u.name, "v_gate_w"),
                       writes=(pk,))
            k.v_red(red[:, 0:1], ps.ap()[:, 0:NE], ALU.max, reads=(pk,),
                    writes=(red.name,), negate=True)
            k.op("scalar", lambda e, o=sm[:, 0:NE], i=ps.ap()[:, 0:NE],
                 b=red[:, 0:1]: e.activation(o, i, AF.Exp, bias=b),
                 reads=(pk, red.name), writes=(sm.name,))
            k.v_red(red[:, 1:2], sm[:, 0:NE], ALU.add, reads=(sm.name,),
                    writes=(red.name,))
            k.v_recip(red[:, 2:3], red[:, 1:2], reads=(red.name,), writes=(red.name,))
            k.v_ts(sc[:, 0:NE], sm[:, 0:NE], red[:, 2:3], None, ALU.mult, None,
                   reads=(sm.name, red.name), writes=(sc.name,))
            k.v_red(red[:, 0:1], sc[:, 0:NE], ALU.max, reads=(sc.name,),
                    writes=(red.name,))
            k.v_ts(ct[:, 0:NE], sc[:, 0:NE], red[:, 0:1], None, ALU.is_ge, None,
                   reads=(sc.name, red.name), writes=(ct.name,))
            k.v_stt(ct[:, 0:NE], ct[:, 0:NE], -1e30, sc[:, 0:NE], ALU.mult, ALU.add,
                    reads=(ct.name, sc.name), writes=(ct.name,))
            k.v_red(red[:, 0:1], ct[:, 0:NE], ALU.max, reads=(ct.name,),
                    writes=(red.name,))
            k.v_ts(ct[:, 0:NE], sc[:, 0:NE], red[:, 0:1], None, ALU.is_ge, None,
                   reads=(sc.name, red.name), writes=(ct.name,))
            k.v_stt(ct[:, 0:NE], sc[:, 0:NE], 0.5, ct[:, 0:NE], ALU.mult, ALU.mult,
                    reads=(sc.name, ct.name), writes=(ct.name,))
            ps2, pk2 = self.next_ps()
            k.t_transpose(ps2.ap()[0:NE, 0:128], ct[:, 0:NE], self.ident[:, 0:128],
                          reads=(ct.name, "ident"), writes=(pk2,))
            k.v_copy(combT[0:NE, tt * 128:(tt + 1) * 128], ps2.ap()[0:NE, 0:128],
                     reads=(pk2,), writes=(combT.name,))
        for b in (sm, sc, r1, ct, red):
            k.free(b.name)
        self.tap_dma(f"combT_{tag}", combT[0:NE, 0:TMC], TMC, (combT.name,))

        combB = k.alloc(f"combB_{tag}", TMC, F32)
        gs = k.alloc(f"gs_{tag}", HT * TMC, BF16)
        silu_t = k.alloc(f"silu_{tag}", TMC, F32)
        wdq = [k.alloc(f"wd_{tag}_{i}", HT * 512, BF16) for i in range(2)]
        wgu_ring = [k.alloc(f"wgu_{tag}_{i}", DT * 256, BF16) for i in range(3)]
        dr = self.dr

        def load_wgu(e, ht):
            slot = wgu_ring[ht % 3]
            self.wblock_dma(slot, 0, dr["wgu"], e * DT, DT, ht * 256, 256,
                            slot.name)

        def load_wdq(e, q):
            buf = wdq[q % 2]
            srcv = dr["w_down"].ap().rearrange("(kt p) m -> p kt m", p=128)[
                :, e * HT:(e + 1) * HT, q * 512:(q + 1) * 512]
            dstv = buf[:, 0:HT * 512].rearrange("p (kt m) -> p kt m", kt=HT)
            k.dma("gpsimd", dstv, srcv, reads=(), writes=(buf.name,))

        for e in range(NE):
            for n in range(NN):
                ps, pk = self.next_ps()
                k.mm_group(ps.ap()[:, n * 512:(n + 1) * 512],
                           [(self.oneh[0:NE, e * 128:(e + 1) * 128],
                             combT[0:NE, n * 512:(n + 1) * 512])],
                           reads=(combT.name, "oneh"), writes=(pk,))
                k.s_act(combB[:, n * 512:(n + 1) * 512],
                        ps.ap()[:, n * 512:(n + 1) * 512], AF.Identity,
                        reads=(pk,), writes=(combB.name,))
            load_wgu(e, 0); load_wgu(e, 1)
            load_wdq(e, 0)
            for ht in range(HT):
                if ht + 2 < HT:
                    load_wgu(e, ht + 2)
                slot = wgu_ring[ht % 3]
                psg, pkg = self.next_ps()
                psu, pku = self.next_ps()
                for n in range(NN):
                    pairs = [(slot[:, kt * 256: kt * 256 + 128],
                              u[:, kt * TMC + n * 512: kt * TMC + n * 512 + 512])
                             for kt in range(DT)]
                    k.mm_group(psg.ap()[:, n * 512:(n + 1) * 512], pairs,
                               reads=(slot.name, u.name), writes=(pkg,))
                    pairs = [(slot[:, kt * 256 + 128: kt * 256 + 256],
                              u[:, kt * TMC + n * 512: kt * TMC + n * 512 + 512])
                             for kt in range(DT)]
                    k.mm_group(psu.ap()[:, n * 512:(n + 1) * 512], pairs,
                               reads=(slot.name, u.name), writes=(pku,))
                k.s_act(silu_t[:, 0:TMC], psg.ap()[:, 0:TMC], AF.Silu,
                        reads=(pkg,), writes=(silu_t.name,))
                k.v_tt(silu_t[:, 0:TMC], silu_t[:, 0:TMC], combB[:, 0:TMC],
                       ALU.mult, reads=(silu_t.name, combB.name),
                       writes=(silu_t.name,))
                k.v_tt(gs[:, ht * TMC:(ht + 1) * TMC], silu_t[:, 0:TMC],
                       psu.ap()[:, 0:TMC], ALU.mult,
                       reads=(silu_t.name, pku), writes=(gs.name,))
            for q in range(4):
                if q + 1 < 4:
                    load_wdq(e, q + 1)
                buf = wdq[q % 2]
                for mi in range(4):
                    mt = q * 4 + mi
                    ps, pk = self.next_ps()
                    for n in range(NN):
                        pairs = [(buf[:, kt * 512 + mi * 128: kt * 512 + mi * 128 + 128],
                                  gs[:, kt * TMC + n * 512: kt * TMC + n * 512 + 512])
                                 for kt in range(HT)]
                        k.mm_group(ps.ap()[:, n * 512:(n + 1) * 512], pairs,
                                   reads=(buf.name, gs.name), writes=(pk,))
                    sl = slice(mt * TMC, (mt + 1) * TMC)
                    xrk = (tile_keys[mt],) if (tile_keys and e == 0) else ()
                    k.v_tt(xb[:, sl], xb[:, sl], ps.ap()[:, 0:TMC], ALU.add,
                           reads=(xkey, pk) + xrk, writes=(xkey,))
        for b in wgu_ring:
            k.free(b.name)
        for b in wdq:
            k.free(b.name)
        k.free(gs.name); k.free(silu_t.name)
        k.free(combB.name); k.free(combT.name)

        sg_buf = k.alloc(f"sg_{tag}", HST * TMC, BF16)
        silu_t2 = k.alloc(f"silu2_{tag}", TMC, F32)
        swg_ring = [k.alloc(f"swg_{tag}_{i}", DT * 256, BF16) for i in range(3)]

        def load_swgu(mt):
            slot = swg_ring[mt % 3]
            self.wblock_dma(slot, 0, dr["swgu"], 0, DT, mt * 256, 256, slot.name)

        load_swgu(0); load_swgu(1)
        for mt in range(HST):
            if mt + 2 < HST:
                load_swgu(mt + 2)
            slot = swg_ring[mt % 3]
            psg, pkg = self.next_ps()
            psu, pku = self.next_ps()
            for n in range(NN):
                pairs = [(slot[:, kt * 256: kt * 256 + 128],
                          u[:, kt * TMC + n * 512: kt * TMC + n * 512 + 512])
                         for kt in range(DT)]
                k.mm_group(psg.ap()[:, n * 512:(n + 1) * 512], pairs,
                           reads=(slot.name, u.name), writes=(pkg,))
                pairs = [(slot[:, kt * 256 + 128: kt * 256 + 256],
                          u[:, kt * TMC + n * 512: kt * TMC + n * 512 + 512])
                         for kt in range(DT)]
                k.mm_group(psu.ap()[:, n * 512:(n + 1) * 512], pairs,
                           reads=(slot.name, u.name), writes=(pku,))
            k.s_act(silu_t2[:, 0:TMC], psg.ap()[:, 0:TMC], AF.Silu,
                    reads=(pkg,), writes=(silu_t2.name,))
            k.v_tt(sg_buf[:, mt * TMC:(mt + 1) * TMC], silu_t2[:, 0:TMC],
                   psu.ap()[:, 0:TMC], ALU.mult,
                   reads=(silu_t2.name, pku), writes=(sg_buf.name,))
        for b in swg_ring:
            k.free(b.name)
        k.free(silu_t2.name)
        swd2 = [k.alloc(f"swd_{tag}_{i}", HST * 256, BF16) for i in range(2)]

        def load_swd(dh):
            self.wblock_dma(swd2[dh % 2], 0, dr["sw_down"], 0, 11, dh * 256,
                            256, swd2[dh % 2].name)
            self.wblock_dma(swd2[dh % 2], 11 * 256, dr["sw_down"], 11, 11,
                            dh * 256, 256, swd2[dh % 2].name)

        load_swd(0)
        for dh in range(8):
            if dh + 1 < 8:
                load_swd(dh + 1)
            swd = swd2[dh % 2]
            for mt8 in range(2):
                mt = dh * 2 + mt8
                ps, pk = self.next_ps()
                for n in range(NN):
                    pairs = [(swd[:, kt * 256 + mt8 * 128: kt * 256 + mt8 * 128 + 128],
                              sg_buf[:, kt * TMC + n * 512: kt * TMC + n * 512 + 512])
                             for kt in range(HST)]
                    k.mm_group(ps.ap()[:, n * 512:(n + 1) * 512], pairs,
                               reads=(swd.name, sg_buf.name), writes=(pk,))
                sl = slice(mt * TMC, (mt + 1) * TMC)
                k.v_stt(xb[:, sl], ps.ap()[:, 0:TMC], 0.5, xb[:, sl],
                        ALU.mult, ALU.add, reads=(pk, xkey),
                        writes=(xkey, f"{xkey}.{tag}.t{mt}"))
        for bu in swd2:
            k.free(bu.name)
        k.free(sg_buf.name)
        k.free(u.name)
        return [f"{xkey}.{tag}.t{mt}" for mt in range(DT)]

    def attn_stage(self, xb, xkey, mc, tag, tile_keys=None):
        k = self.k
        dr = self.dr
        vec = self.vec
        rT = slice(0, TMC)
        sum_row, ssq_row = self.stats_rows(
            xb, xkey, True, tag,
            tile_keys=tile_keys if getattr(build_core, "TK_STATS", True) else None)
        m_row = k.alloc(f"row_m_{tag}", TMC, F32)
        v_row = k.alloc(f"row_v_{tag}", TMC, F32)
        s1_row = k.alloc(f"row_s1_{tag}", TMC, F32)
        sc_row = k.alloc(f"row_sc_{tag}", TMC, F32)
        t_row = k.alloc(f"row_t_{tag}", TMC, F32)
        k.v_ts(m_row[0:1, rT], sum_row[0:1, rT], 1.0 / D, None, ALU.mult, None,
               reads=(sum_row.name,), writes=(m_row.name,))
        k.v_tt(v_row[0:1, rT], m_row[0:1, rT], m_row[0:1, rT], ALU.mult,
               reads=(m_row.name,), writes=(v_row.name,))
        k.v_stt(v_row[0:1, rT], ssq_row[0:1, rT], 1.0 / D, v_row[0:1, rT],
                ALU.mult, ALU.subtract, reads=(ssq_row.name, v_row.name),
                writes=(v_row.name,))
        self.rstd_row(s1_row[0:1, rT], s1_row.name, v_row[0:1, rT], v_row.name,
                      0, tag)
        k.v_ts(t_row[0:1, rT], v_row[0:1, rT], EPS1, None, ALU.add, None,
               reads=(v_row.name,), writes=(t_row.name,))
        k.v_recip(t_row[0:1, rT], t_row[0:1, rT], reads=(t_row.name,),
                  writes=(t_row.name,))
        k.v_tt(t_row[0:1, rT], v_row[0:1, rT], t_row[0:1, rT], ALU.mult,
               reads=(v_row.name, t_row.name), writes=(t_row.name,))
        self.rstd_row(sc_row[0:1, rT], sc_row.name, t_row[0:1, rT], t_row.name,
                      1, tag)
        k.v_tt(sc_row[0:1, rT], sc_row[0:1, rT], s1_row[0:1, rT], ALU.mult,
               reads=(sc_row.name, s1_row.name), writes=(sc_row.name,))
        m1B = self.bcast_row(m_row, m_row.name, f"m1_{tag}")
        s1B = self.bcast_row(s1_row, s1_row.name, f"s1_{tag}")
        scB = self.bcast_row(sc_row, sc_row.name, f"sc_{tag}")
        for b in (sum_row, ssq_row, m_row, v_row, s1_row, sc_row, t_row):
            k.free(b.name)
        xn = k.alloc(f"xn_{tag}", DT * TMC, BF16)
        tn = k.alloc(f"tn_{tag}", TMC, F32)
        for kt in range(DT):
            sl = slice(kt * TMC, (kt + 1) * TMC)
            xrk = (tile_keys[kt] if (tile_keys and
                   getattr(build_core, "TK_XN", True)) else xkey)
            k.v_tt(tn[:, 0:TMC], xb[:, sl], m1B[:, 0:TMC], ALU.subtract,
                   reads=(xrk, m1B.name), writes=(tn.name,))
            k.v_stt(xn[:, sl], tn[:, 0:TMC], vec["norm1_w"][:, kt:kt + 1],
                    scB[:, 0:TMC], ALU.mult, ALU.mult,
                    reads=(tn.name, scB.name, "v_norm1_w"), writes=(xn.name,))
        k.free(tn.name)
        k.free(scB.name)
        self.tap_dma(f"xn_{tag}", xn[:, 0:DT * TMC], DT * TMC, (xn.name,), BF16)
        xsp3 = self.xsp.ap().rearrange("p (kt t) -> p kt t", kt=DT)
        xb3 = xb[:, 0:DT * TMC].rearrange("p (kt t) -> p kt t", kt=DT)
        k.dma("sync", xsp3, xb3,
              reads=(xkey,) + (tuple(tile_keys) if tile_keys else ()),
              writes=("xspill",))
        k.free(xb.name)

        vbuf = k.alloc(f"v_{tag}", DT * TMC, F32)
        left = k.alloc(f"left_{tag}", DT * TMC, BF16)
        cbf = k.alloc(f"cbf_{tag}", DT * TMC, BF16)
        ctmp = k.alloc(f"ctmp_{tag}", TMC, F32)
        cw = vec["conv_w"]
        cb = vec["conv_b"]
        halo = self.vt0 if mc == 0 else self.vt1
        halokey = "vt0" if mc == 0 else "vt1"
        rl_ring = [k.alloc(f"rl_{tag}_{i}", DT * 256, BF16) for i in range(3)]

        def load_rl(mt):
            slot = rl_ring[mt % 3]
            self.wblock_dma(slot, 0, dr["rlW"], 0, DT, mt * 256, 256, slot.name)

        def rhs_xn(kt, n):
            return xn[:, kt * TMC + n * 512: kt * TMC + n * 512 + 512]

        load_rl(0); load_rl(1)
        for mt in range(DT):
            if mt + 2 < DT:
                load_rl(mt + 2)
            slot = rl_ring[mt % 3]
            sl = slice(mt * TMC, (mt + 1) * TMC)
            psv, pkv = self.next_ps()
            psl, pkl = self.next_ps()
            for n in range(NN):
                pairs = [(slot[:, kt * 256: kt * 256 + 128], rhs_xn(kt, n))
                         for kt in range(DT)]
                k.mm_group(psv.ap()[:, n * 512:(n + 1) * 512], pairs,
                           reads=(slot.name, xn.name), writes=(pkv,))
                pairs = [(slot[:, kt * 256 + 128: kt * 256 + 256], rhs_xn(kt, n))
                         for kt in range(DT)]
                k.mm_group(psl.ap()[:, n * 512:(n + 1) * 512], pairs,
                           reads=(slot.name, xn.name), writes=(pkl,))
            k.s_act(vbuf[:, sl], psv.ap()[:, 0:TMC], AF.Identity,
                    reads=(pkv, "v_right_b"), writes=(vbuf.name,),
                    bias=vec["right_b"][:, mt:mt + 1])
            k.s_act(left[:, sl], psl.ap()[:, 0:TMC], AF.Gelu,
                    reads=(pkl, "v_left_b"), writes=(left.name,),
                    bias=vec["left_b"][:, mt:mt + 1])
            if mc == 0:
                k.v_copy(self.vt1[:, mt * 3:mt * 3 + 3],
                         vbuf[:, mt * TMC + TMC - 3: mt * TMC + TMC],
                         reads=(vbuf.name,), writes=("vt1",))
            k.v_ts(ctmp[:, 0:TMC], vbuf[:, sl], cw[:, mt * 4 + 3:mt * 4 + 4],
                   cb[:, mt:mt + 1], ALU.mult, ALU.add,
                   reads=(vbuf.name, "v_conv_w", "v_conv_b"), writes=(ctmp.name,))
            for s in range(1, 4):
                w_ap = cw[:, mt * 4 + 3 - s:mt * 4 + 4 - s]
                k.v_stt(ctmp[:, s:TMC], vbuf[:, mt * TMC: mt * TMC + TMC - s],
                        w_ap, ctmp[:, s:TMC], ALU.mult, ALU.add,
                        reads=(vbuf.name, "v_conv_w", ctmp.name),
                        writes=(ctmp.name,))
                k.v_stt(ctmp[:, 0:s], halo[:, mt * 3 + 3 - s: mt * 3 + 3],
                        w_ap, ctmp[:, 0:s], ALU.mult, ALU.add,
                        reads=(halokey, "v_conv_w", ctmp.name),
                        writes=(ctmp.name,))
            k.v_copy(vbuf[:, sl], ctmp[:, 0:TMC], reads=(ctmp.name,),
                     writes=(vbuf.name,))
            k.v_copy(cbf[:, sl], vbuf[:, sl], reads=(vbuf.name,),
                     writes=(cbf.name,))
        for bu in rl_ring:
            k.free(bu.name)
        k.free(ctmp.name)
        k.free(xn.name)
        cbuf = vbuf
        self.tap_dma(f"c_{tag}", cbuf[:, 0:DT * TMC], DT * TMC, (cbuf.name,))

        lr_all = cbuf.ap().bitcast(BF16)
        A = k.alloc(f"gA_{tag}", TMC, F32)
        Bv = k.alloc(f"gB_{tag}", TMC, F32)
        Cv = k.alloc(f"gC_{tag}", TMC, F32)
        Dv = k.alloc(f"gD_{tag}", TMC, F32)
        Ev = k.alloc(f"gE_{tag}", TMC, F32)
        gring = [k.alloc(f"gw_{tag}_{i}", DT * 256, BF16) for i in range(3)]

        def rhs_c(kt, n):
            return cbf[:, kt * TMC + n * 512: kt * TMC + n * 512 + 512]

        def load_gagx(mt):
            slot = gring[mt % 3]
            self.wblock_dma(slot, 0, dr["gagx"], 0, DT, mt * 256, 256, slot.name)

        load_gagx(0); load_gagx(1)
        for mt in range(DT):
            if mt + 2 < DT:
                load_gagx(mt + 2)
            slot = gring[mt % 3]
            psa, pka = self.next_ps()
            psb, pkb = self.next_ps()
            for n in range(NN):
                pairs = [(slot[:, kt * 256: kt * 256 + 128], rhs_c(kt, n))
                         for kt in range(DT)]
                k.mm_group(psa.ap()[:, n * 512:(n + 1) * 512], pairs,
                           reads=(slot.name, cbf.name), writes=(pka,))
                pairs = [(slot[:, kt * 256 + 128: kt * 256 + 256], rhs_c(kt, n))
                         for kt in range(DT)]
                k.mm_group(psb.ap()[:, n * 512:(n + 1) * 512], pairs,
                           reads=(slot.name, cbf.name), writes=(pkb,))
            k.s_act(A[:, 0:TMC], psa.ap()[:, 0:TMC], AF.Tanh,
                    reads=(pka, "v_ga_b_h"), writes=(A.name,),
                    bias=vec["ga_b_h"][:, mt:mt + 1], scale=0.5)
            k.s_act(Bv[:, 0:TMC], psb.ap()[:, 0:TMC], AF.Tanh,
                    reads=(pkb, "v_gx_b_h"), writes=(Bv.name,),
                    bias=vec["gx_b_h"][:, mt:mt + 1], scale=0.5)
            k.v_ts(A[:, 0:TMC], A[:, 0:TMC], self.m4sp[:, mt:mt + 1],
                   self.m4sp[:, mt:mt + 1], ALU.mult, ALU.add,
                   reads=(A.name, "v_m4sp"), writes=(A.name,))
            k.s_act(Cv[:, 0:TMC], A[:, 0:TMC], AF.Exp, reads=(A.name,),
                    writes=(Cv.name,))
            k.s_act(A[:, 0:TMC], Cv[:, 0:TMC], AF.Square, reads=(Cv.name,),
                    writes=(A.name,))
            k.v_ts(A[:, 0:TMC], A[:, 0:TMC], -1.0, 1.0, ALU.mult, ALU.add,
                   reads=(A.name,), writes=(A.name,))
            k.v_ts(A[:, 0:TMC], A[:, 0:TMC], 0.0, None, ALU.max, None,
                   reads=(A.name,), writes=(A.name,))
            k.s_act(A[:, 0:TMC], A[:, 0:TMC], AF.Ln, reads=(A.name,),
                    writes=(A.name,))
            k.s_act(A[:, 0:TMC], A[:, 0:TMC], AF.Exp, reads=(A.name,),
                    writes=(A.name,), scale=0.5)
            k.v_ts(Bv[:, 0:TMC], Bv[:, 0:TMC], 0.5, 0.5, ALU.mult, ALU.add,
                   reads=(Bv.name,), writes=(Bv.name,))
            k.v_tt(Dv[:, 0:TMC], Bv[:, 0:TMC],
                   cbuf[:, mt * TMC:(mt + 1) * TMC], ALU.mult,
                   reads=(Bv.name, cbuf.name), writes=(Dv.name,))
            k.v_tt(Dv[:, 0:TMC], A[:, 0:TMC], Dv[:, 0:TMC], ALU.mult,
                   reads=(A.name, Dv.name), writes=(Dv.name,))
            k.v_scan(Ev[:, 0:TMC], Cv[:, 0:TMC], Dv[:, 0:TMC],
                     self.hcar[:, mt:mt + 1], reads=(Cv.name, Dv.name, "hcar"),
                     writes=(Ev.name,))
            k.v_copy(self.hcar[:, mt:mt + 1], Ev[:, TMC - 1:TMC],
                     reads=(Ev.name,), writes=("hcar",))
            lr_mt = lr_all[:, mt * 2 * TMC: mt * 2 * TMC + TMC]
            k.v_tt(lr_mt, left[:, mt * TMC:(mt + 1) * TMC], Ev[:, 0:TMC],
                   ALU.mult, reads=(left.name, Ev.name), writes=("lr",))
            if mt == DT - 1:
                self.tap_dma(f"ga_{tag}", Cv[:, 0:TMC], TMC, (Cv.name,))
                self.tap_dma(f"gb_{tag}", Dv[:, 0:TMC], TMC, (Dv.name,))
                self.tap_dma(f"gh_{tag}", Ev[:, 0:TMC], TMC, (Ev.name,))
                self.tap_dma(f"gl_{tag}", left[:, mt * TMC:(mt + 1) * TMC], TMC,
                             (left.name,), BF16)
        for b in (A, Bv, Cv, Dv, Ev):
            k.free(b.name)
        for b in gring:
            k.free(b.name)
        k.free(cbf.name)
        k.free(left.name)

        if getattr(build_core, "DBG_BARRIER", "") == "preow":
            k.full_barrier()
        x1 = k.alloc(f"x1_{tag}", DT * TMC, F32)

        def rhs_lr(kt, n):
            return lr_all[:, kt * 2 * TMC + n * 512: kt * 2 * TMC + n * 512 + 512]

        def evac_o(mt, ps, pk):
            k.s_act(x1[:, mt * TMC:(mt + 1) * TMC], ps.ap()[:, 0:TMC],
                    AF.Identity, reads=(pk, "v_out_b"),
                    writes=(x1.name, f"{x1.name}.e{mt}"),
                    bias=vec["out_b"][:, mt:mt + 1])

        self.mm_stream(dr["out_W"], DT, rhs_lr, evac_o, f"ow_{tag}",
                       ring_slots=4, rhs_keys=("lr",))
        k.alias_key(cbuf.name, "lr")
        k.free(cbuf.name)
        if getattr(build_core, "DBG_BARRIER", "") == "postow":
            k.full_barrier()

        xres = k.alloc(f"xres_{tag}", DT * TMC, F32, high=True)
        xsp3 = self.xsp.ap().rearrange("p (kt t) -> p kt t", kt=DT)
        xres3 = xres[:, 0:DT * TMC].rearrange("p (kt t) -> p kt t", kt=DT)
        for q in range(4):
            k.dma("sync", xres3[:, 4 * q:4 * q + 4, :],
                  xsp3[:, 4 * q:4 * q + 4, :],
                  reads=("xspill",), writes=(xres.name, f"{xres.name}.q{q}"))
        self.tap_dma(f"x1pre_{tag}", x1[:, 0:DT * TMC], DT * TMC, (x1.name,))
        self.tap_dma(f"xres0_{tag}", xres[:, 0:DT * TMC], DT * TMC,
                     (xres.name, f"{xres.name}.q0", f"{xres.name}.q1",
                      f"{xres.name}.q2", f"{xres.name}.q3"))
        tn2 = k.alloc(f"tn2_{tag}", TMC, F32)
        tn2b = k.alloc(f"tn2b_{tag}", TMC, F32)
        x1tk = []
        xrtk = []
        for kt in range(DT):
            sl = slice(kt * TMC, (kt + 1) * TMC)
            qk = f"{xres.name}.q{kt // 4}"
            ek = f"{x1.name}.e{kt}"
            tkey = f"{x1.name}.t{kt}"
            rkey = f"{xres.name}.t{kt}"
            x1tk.append(tkey)
            xrtk.append(rkey)
            t_ = tn2
            opf = k.v_tt
            opf(t_[:, 0:TMC], xres[:, sl], m1B[:, 0:TMC], ALU.subtract,
                reads=(qk, m1B.name), writes=(t_.name,))
            opf(t_[:, 0:TMC], t_[:, 0:TMC], s1B[:, 0:TMC], ALU.mult,
                reads=(t_.name, s1B.name), writes=(t_.name,))
            opf(x1[:, sl], x1[:, sl], t_[:, 0:TMC], ALU.add,
                reads=(ek, t_.name), writes=(tkey,))
            opf(xres[:, sl], xres[:, sl], x1[:, sl], ALU.add,
                reads=(qk, tkey), writes=(rkey,))
        k.free(tn2.name)
        k.free(tn2b.name)
        k.free(m1B.name)
        k.free(s1B.name)
        self.tap_dma(f"x1_{tag}", x1[:, 0:DT * TMC], DT * TMC, (x1.name,))

        _, ssq2 = self.stats_rows(x1, x1.name, False, tag + "r2",
                                  tile_keys=x1tk)
        s2_row = k.alloc(f"row_s2_{tag}", TMC, F32)
        k.v_ts(s2_row[0:1, rT], ssq2[0:1, rT], 1.0 / D, None, ALU.mult, None,
               reads=(ssq2.name,), writes=(s2_row.name,))
        self.rstd_row(s2_row[0:1, rT], s2_row.name, s2_row[0:1, rT], s2_row.name,
                      1, tag)
        s2B = self.bcast_row(s2_row, s2_row.name, f"s2_{tag}")
        k.free(ssq2.name); k.free(s2_row.name)
        uring = [k.alloc(f"ur_{tag}_{i}", DT * 256, BF16) for i in range(3)]

        def load_up12(ft):
            slot = uring[ft % 3]
            self.wblock_dma(slot, 0, dr["up12"], 0, DT, ft * 256, 256, slot.name)

        load_up12(0); load_up12(1)
        x1n = k.alloc(f"x1n_{tag}", DT * TMC, BF16)
        tn3 = k.alloc(f"tn3_{tag}", TMC, F32)
        for kt in range(DT):
            sl = slice(kt * TMC, (kt + 1) * TMC)
            k.v_tt(tn3[:, 0:TMC], x1[:, sl], s2B[:, 0:TMC], ALU.mult,
                   reads=(x1tk[kt], s2B.name), writes=(tn3.name,))
            k.v_ts(x1n[:, sl], tn3[:, 0:TMC], vec["norm2_w"][:, kt:kt + 1],
                   None, ALU.mult, None, reads=(tn3.name, "v_norm2_w"),
                   writes=(x1n.name,))
        k.free(tn3.name); k.free(s2B.name)
        for tk in x1tk:
            k.alias_key(x1.name, tk)
        k.free(x1.name)

        gq = k.alloc(f"gq_{tag}", FQT * TMC, BF16)
        dslab = k.alloc(f"dslab_{tag}", FQT * D, BF16)
        tgel = k.alloc(f"tgel_{tag}", TMC, F32)

        def rhs_x1n(kt, n):
            return x1n[:, kt * TMC + n * 512: kt * TMC + n * 512 + 512]

        def load_dslab(fq):
            for mh in range(2):
                srcv = dr["down_W"].ap().rearrange("(kt p) m -> p kt m", p=128)[
                    :, fq * FQT:(fq + 1) * FQT, mh * 1024:(mh + 1) * 1024]
                dstv = dslab[:, 0:FQT * D].rearrange("p (kt m) -> p kt m",
                                                     kt=FQT)[
                    :, :, mh * 1024:(mh + 1) * 1024]
                k.dma("gpsimd", dstv, srcv, reads=(), writes=(dslab.name,))

        for fq in range(FQ):
            if fq > 0:
                load_up12(fq * FQT); load_up12(fq * FQT + 1)
            load_dslab(fq)
            for ml in range(FQT):
                ft = fq * FQT + ml
                if ml + 2 < FQT:
                    load_up12(ft + 2)
                slot = uring[ft % 3]
                ps1, pk1 = self.next_ps()
                ps2, pk2 = self.next_ps()
                for n in range(NN):
                    pairs = [(slot[:, kt * 256: kt * 256 + 128], rhs_x1n(kt, n))
                             for kt in range(DT)]
                    k.mm_group(ps1.ap()[:, n * 512:(n + 1) * 512], pairs,
                               reads=(slot.name, x1n.name), writes=(pk1,))
                    pairs = [(slot[:, kt * 256 + 128: kt * 256 + 256],
                              rhs_x1n(kt, n)) for kt in range(DT)]
                    k.mm_group(ps2.ap()[:, n * 512:(n + 1) * 512], pairs,
                               reads=(slot.name, x1n.name), writes=(pk2,))
                k.s_act(tgel[:, 0:TMC], ps1.ap()[:, 0:TMC], AF.Gelu,
                        reads=(pk1, "v_up1_b"), writes=(tgel.name,),
                        bias=vec["up1_b"][:, ft:ft + 1])
                k.v_stt(gq[:, ml * TMC:(ml + 1) * TMC], ps2.ap()[:, 0:TMC],
                        vec["up2_b"][:, ft:ft + 1], tgel[:, 0:TMC],
                        ALU.add, ALU.mult,
                        reads=(pk2, "v_up2_b", tgel.name), writes=(gq.name,))
            for mt in range(DT):
                ps, pk = self.next_ps()
                for n in range(NN):
                    pairs = [(dslab[:, kt * D + mt * 128: kt * D + mt * 128 + 128],
                              gq[:, kt * TMC + n * 512: kt * TMC + n * 512 + 512])
                             for kt in range(FQT)]
                    k.mm_group(ps.ap()[:, n * 512:(n + 1) * 512], pairs,
                               reads=(dslab.name, gq.name), writes=(pk,))
                sl = slice(mt * TMC, (mt + 1) * TMC)
                rkey = f"{xres.name}.t{mt}"
                if fq == 0:
                    k.v_stt(xres[:, sl], ps.ap()[:, 0:TMC],
                            vec["down_b"][:, mt:mt + 1], xres[:, sl],
                            ALU.add, ALU.add,
                            reads=(pk, "v_down_b", rkey), writes=(rkey,))
                else:
                    k.v_tt(xres[:, sl], xres[:, sl], ps.ap()[:, 0:TMC], ALU.add,
                           reads=(rkey, pk), writes=(rkey,))
        for b in uring:
            k.free(b.name)
        k.free(gq.name); k.free(dslab.name); k.free(tgel.name)
        k.free(x1n.name)
        return xres, xrtk


def build_core():
    from contextlib import ExitStack
    nc = bass.Bass("TRN2", target_bir_lowering=False)
    es = ExitStack()
    k = K(nc)
    m = Model(nc, k)
    m.init_psum(es)

    dr = {}
    dr["xin"] = nc.dram_tensor("xin", [128, DT * T], F32, kind="ExternalInput")
    for nm, shp in [("rlW", [D, 2 * D]),
                    ("gagx", [D, 2 * D]), ("out_W", [D, D]),
                    ("up12", [D, 2 * FF]), ("down_W", [FF, D]),
                    ("wgu", [NE * D, 2 * HE]),
                    ("w_down", [NE * HE, D]),
                    ("swgu", [D, 2 * HS]), ("sw_down", [HS, D])]:
        dr[nm] = nc.dram_tensor(nm, shp, F32, kind="ExternalInput")
    for nm, n in [("norm1_w", DT), ("norm2_w", DT), ("left_b", DT),
                  ("right_b", DT), ("out_b", DT), ("conv_b", DT),
                  ("down_b", DT), ("up1_b", FT), ("up2_b", FT),
                  ("ga_b", DT), ("gx_b", DT), ("a_param", DT),
                  ("conv_w", DT * 4), ("gate_w", DT * NE)]:
        dr[nm] = nc.dram_tensor(nm, [128, n], F32, kind="ExternalInput")
    dr["out"] = nc.dram_tensor("out", [128, DT * T], F32, kind="ExternalOutput")
    m.xsp = nc.dram_tensor("xsp", [128, DT * TMC], F32, kind="Internal")
    m.dr = dr

    m.init_constants()
    m.init_vectors(dr)
    m.vt0 = k.alloc("vt0", DT * 3, F32)
    m.vt1 = k.alloc("vt1", DT * 3, F32)
    m.hcar = k.alloc("hcar", DT, F32)
    k.v_memset(m.vt0[:, 0:DT * 3], 0.0, reads=(), writes=("vt0",))
    k.v_memset(m.hcar[:, 0:DT], 0.0, reads=(), writes=("hcar",))

    out3 = dr["out"].ap().rearrange("p (kt t) -> p kt t", kt=DT)
    xin3 = dr["xin"].ap().rearrange("p (kt t) -> p kt t", kt=DT)
    out_tokens = []
    n_mc = getattr(build_core, "N_MC", NMC)
    stages = getattr(build_core, "STAGES", ("ffn1", "attn", "ffn2"))

    def load_x(mc):
        xb = k.alloc(f"x_mc{mc}", DT * TMC, F32, high=True)
        xb3 = xb[:, 0:DT * TMC].rearrange("p (kt t) -> p kt t", kt=DT)
        tkeys = []
        for q in range(4):
            qk = f"{xb.name}.q{q}"
            k.dma("sync", xb3[:, 4 * q:4 * q + 4, :],
                  xin3[:, 4 * q:4 * q + 4, mc * TMC:(mc + 1) * TMC],
                  reads=(), writes=(xb.name, qk))
            tkeys.extend([qk] * 4)
        return xb, tkeys

    xb, tkeys = load_x(0)
    for mc in range(n_mc):
        xkey = xb.name
        if "ffn1" in stages:
            tkeys = m.ffn_stage(xb, xkey, mc, f"f1m{mc}", tile_keys=tkeys)
        if "attn" in stages:
            xb, tkeys = m.attn_stage(xb, xkey, mc, f"atm{mc}", tile_keys=tkeys)
            xkey = xb.name
        if "ffn2" in stages:
            tkeys = m.ffn_stage(xb, xkey, mc, f"f2m{mc}", tile_keys=tkeys)
        nxt = load_x(mc + 1) if mc + 1 < n_mc else None
        tok = k.dma("sync", out3[:, :, mc * TMC:(mc + 1) * TMC],
                    xb[:, 0:DT * TMC].rearrange("p (kt t) -> p kt t", kt=DT),
                    reads=(xkey,) + tuple(tkeys), writes=("outdram",))
        out_tokens.append(tok)
        for tk in tkeys:
            k.alias_key(xb.name, tk)
        k.free(xb.name)
        if nxt is not None:
            xb, tkeys = nxt
    k.barrier_wait("sync", out_tokens)
    k.build_and_emit(es)
    return nc, es, m



def _prep_vec(v):
    M = v.shape[0]
    return np.ascontiguousarray(v.reshape(M // 128, 128).T).astype(np.float32)


def _host_inputs(inputs):
    x = np.asarray(inputs["L0_attn_x"], dtype=np.float32)
    g = lambda n: np.asarray(inputs[n], dtype=np.float32)
    def merge_cols(wa, wb):
        R, M = wa.shape
        nt = M // 128
        out = np.concatenate([wa.reshape(R, nt, 128), wb.reshape(R, nt, 128)],
                             axis=2)
        return np.ascontiguousarray(out.reshape(R, 2 * M))

    wmats = {
        "rlW": merge_cols(g("L0_attn_right_W"), g("L0_attn_left_W")),
        "gagx": merge_cols(g("L0_attn_ga_W"), g("L0_attn_gx_W")),
        "out_W": g("L0_attn_out_W"),
        "up12": merge_cols(g("L0_attn_up1_W"), g("L0_attn_up2_W")),
        "down_W": g("L0_attn_down_W"),
        "wgu": merge_cols(g("L0_ffn_w_gate").reshape(NE * D, HE),
                          g("L0_ffn_w_up").reshape(NE * D, HE)),
        "w_down": np.ascontiguousarray(g("L0_ffn_w_down").reshape(NE * HE, D)),
        "swgu": merge_cols(g("L0_ffn_sw_gate"), g("L0_ffn_sw_up")),
        "sw_down": g("L0_ffn_sw_down"),
    }
    vecs = {
        "norm1_w": _prep_vec(g("L0_attn_norm1_w")),
        "norm2_w": _prep_vec(g("L0_attn_norm2_w")),
        "left_b": _prep_vec(g("L0_attn_left_b")),
        "right_b": _prep_vec(g("L0_attn_right_b")),
        "out_b": _prep_vec(g("L0_attn_out_b")),
        "conv_b": _prep_vec(g("L0_attn_conv_b")),
        "down_b": _prep_vec(g("L0_attn_down_b")),
        "up1_b": _prep_vec(g("L0_attn_up1_b")),
        "up2_b": _prep_vec(g("L0_attn_up2_b")),
        "ga_b": _prep_vec(g("L0_attn_ga_b")),
        "gx_b": _prep_vec(g("L0_attn_gx_b")),
        "a_param": _prep_vec(g("L0_attn_a_param")),
        "conv_w": np.ascontiguousarray(
            g("L0_attn_conv_w").reshape(4, DT, 128).transpose(2, 1, 0)
        ).reshape(128, DT * 4).astype(np.float32),
        "gate_w": np.ascontiguousarray(
            g("L0_ffn_gate_w").reshape(DT, 128, NE).transpose(1, 0, 2)
        ).reshape(128, DT * NE).astype(np.float32),
    }
    in_maps = []
    for b in range(NCORES):
        xb = np.ascontiguousarray(
            x[b].reshape(T, DT, 128).transpose(2, 1, 0)).reshape(128, DT * T)
        im = {"xin": xb}
        im.update(wmats)
        im.update(vecs)
        in_maps.append(im)
    return in_maps


def _host_output(res_list):
    outs = []
    for r in res_list:
        o = r["out"].reshape(128, DT, T).transpose(2, 1, 0).reshape(T, D)
        outs.append(o)
    return np.ascontiguousarray(np.stack(outs, axis=0)).astype(np.float32)


def kernel(**inputs):
    in_maps = _host_inputs(inputs)
    nc, es, m = build_core()
    try:
        res = run_bass_kernel_spmd(nc, in_maps, core_ids=list(range(NCORES)))
        return _host_output(res.results)
    finally:
        es.close()


# version 15
# speedup vs baseline: 1.0211x; 1.0211x over previous
import numpy as np
import concourse.bass as bass
import concourse.mybir as mybir
from concourse.bass_utils import run_bass_kernel_spmd

F32 = mybir.dt.float32
BF16 = mybir.dt.bfloat16
AF = mybir.ActivationFunctionType
ALU = mybir.AluOpType
AXX = mybir.AxisListType.X

NCORES = 8
D = 2048
T = 2048
TMC = 1024
NMC = T // TMC
DT = D // 128
NN = TMC // 512
HE = 1408
HT = HE // 128
NE = 8
HS = 2816
HST = HS // 128
FF = 6144
FT = FF // 128
FQ = 4
FQT = FT // FQ
EPS1 = 1e-5
EPS2 = 1e-8
CRG = 8.0

ENGS = ("sync", "scalar", "vector", "tensor", "gpsimd")

_DEBUG_TAPS = []

def set_debug_taps(taps):
    global _DEBUG_TAPS
    _DEBUG_TAPS = list(taps)



class Arena:

    def __init__(self, base, top):
        self.free_list = [(base, top - base)]
        self.live = {}
        self.dead = []

    def alloc(self, name, nbytes, align=32, high=False):
        nbytes = (nbytes + align - 1) & ~(align - 1)
        order = (range(len(self.free_list) - 1, -1, -1) if high
                 else range(len(self.free_list)))
        for i in order:
            off, size = self.free_list[i]
            if high:
                aoff = (off + size - nbytes) & ~(align - 1)
                if aoff < off:
                    continue
                newfree = []
                if aoff > off:
                    newfree.append((off, aoff - off))
                tail = (off + size) - (aoff + nbytes)
                if tail:
                    newfree.append((aoff + nbytes, tail))
            else:
                aoff = (off + align - 1) & ~(align - 1)
                pad = aoff - off
                if size - pad < nbytes:
                    continue
                rest = size - pad - nbytes
                newfree = []
                if pad:
                    newfree.append((off, pad))
                if rest:
                    newfree.append((aoff + nbytes, rest))
            self.free_list[i:i + 1] = newfree
            self.live[name] = (aoff, nbytes)
            hazards = []
            newdead = []
            for (o, s, n) in self.dead:
                if o + s <= aoff or o >= aoff + nbytes:
                    newdead.append((o, s, n))
                    continue
                hazards.append(n)
                if o < aoff:
                    newdead.append((o, aoff - o, n))
                if o + s > aoff + nbytes:
                    newdead.append((aoff + nbytes, o + s - (aoff + nbytes), n))
            self.dead = newdead
            return aoff, hazards
        raise MemoryError(
            f"SBUF arena: cannot alloc {name} ({nbytes}B); "
            f"free={self.free_list} live={ {k: v for k, v in self.live.items()} }")

    def free(self, name):
        off, size = self.live.pop(name)
        self.dead.append((off, size, name))
        self.free_list.append((off, size))
        self.free_list.sort()
        out = []
        for off, size in self.free_list:
            if out and out[-1][0] + out[-1][1] == off:
                out[-1] = (out[-1][0], out[-1][1] + size)
            else:
                out.append((off, size))
        self.free_list = out


class Buf:
    __slots__ = ("name", "h", "dtype", "nelem")

    def __init__(self, name, h, dtype, nelem):
        self.name = name
        self.h = h
        self.dtype = dtype
        self.nelem = nelem

    def ap(self):
        return self.h.ap()

    def __getitem__(self, sl):
        return self.h.ap()[sl]


class K:

    def __init__(self, nc):
        self.nc = nc
        self.ops = {e: [] for e in ENGS}
        self.cnt = {e: 0 for e in ENGS}
        self.dq = {"sync": 0, "gpsimd": 0, "scalar": 0}
        self.NDSLOT = {'gpsimd': 8, 'sync': 8, 'scalar': 8}
        self.dslot = {}
        self.last_w = {}
        self.readers = {}
        self.waited = {e: {} for e in ENGS}
        base = (nc.sbuf_base + 31) & ~31
        self.arena = Arena(base, nc.sbuf_top)
        self.bufs = {}
        self._uid = 0
        self.sems = {}

    def alloc(self, name, nelem, dtype, high=False):
        eb = 2 if dtype == BF16 else 4
        off, hazards = self.arena.alloc(name, nelem * eb, high=high)
        h = self.nc.alloc_sbuf_tensor_at(name + f"_{self._uid}", [128, nelem], dtype,
                                         offset=off)
        self._uid += 1
        b = Buf(name, h, dtype, nelem)
        self.bufs[name] = b
        toks = []
        for hn in hazards:
            if hn in self.last_w:
                toks.append(self.last_w[hn])
            toks.extend(self.readers.get(hn, ()))
        if toks:
            self.readers.setdefault(name, []).extend(toks)
        return b

    def free(self, name):
        pref = name + "."
        seen = set()
        for kk in list(self.last_w.keys()):
            if kk.startswith(pref):
                self.alias_key(name, kk)
                seen.add(kk)
        for kk in list(self.readers.keys()):
            if kk.startswith(pref) and kk not in seen:
                self.alias_key(name, kk)
        self.arena.free(name)
        del self.bufs[name]

    def alias_key(self, bufname, key):
        toks = list(self.readers.get(key, ()))
        t = self.last_w.get(key)
        if t:
            toks.append(t)
        self.readers.setdefault(bufname, []).extend(toks)

    def _dep_tokens(self, reads, writes):
        toks = []
        for k in reads:
            t = self.last_w.get(k)
            if t:
                toks.append(t)
        for k in writes:
            t = self.last_w.get(k)
            if t:
                toks.append(t)
            toks.extend(self.readers.get(k, ()))
        return toks

    def _commit(self, token, reads, writes):
        for k in writes:
            self.last_w[k] = token
            self.readers[k] = []
        for k in reads:
            self.readers.setdefault(k, []).append(token)

    def _emit_waits(self, engine, toks):
        need = {}
        for t in toks:
            kind, src, val = t
            if kind == "E" and src == engine:
                pass
            cur = need.get((kind, src), 0)
            if val > cur:
                need[(kind, src)] = val
        out = []
        for (kind, src), val in need.items():
            w = self.waited[engine]
            key = (kind, src)
            if w.get(key, 0) >= val:
                continue
            w[key] = val
            if kind == "E":
                out.append((src, val))
            else:
                out.append(("dma_" + src, 16 * val))
        return out

    def op(self, engine, fn, reads=(), writes=(), n_inc=1):
        toks = self._dep_tokens(reads, writes)
        waits = self._emit_waits(engine, toks)
        self.cnt[engine] += 1
        myval = self.cnt[engine]
        token = ("E", engine, myval)
        sem_of = self.sems

        def closure(eng, fn=fn, waits=waits, engine=engine, myval=myval):
            for sname, val in waits:
                eng.wait_ge(sem_of[sname], val)
            inst = fn(eng)
            inst.then_inc(sem_of[engine], 1)

        self.ops[engine].append(closure)
        self._commit(token, reads, writes)
        return token

    def dma(self, queue, out_ap, in_ap, reads=(), writes=(), **kw):
        toks = self._dep_tokens(reads, writes)
        slot = self.dq[queue] % self.NDSLOT[queue]
        self.dq[queue] += 1
        qs = f"{queue}:{slot}"
        prev = self.dslot.get(qs, 0)
        waits = self._emit_waits(queue, toks)
        self.dslot[qs] = prev + 1
        token = ("D", qs, self.dslot[qs])
        sem_of = self.sems

        def closure(eng, out_ap=out_ap, in_ap=in_ap, waits=waits, qs=qs, kw=kw):
            for sname, val in waits:
                eng.wait_ge(sem_of[sname], val)
            eng.dma_start(out=out_ap, in_=in_ap, **kw).then_inc(
                sem_of["dma_" + qs], 16)

        self.ops[queue].append(closure)
        self._commit(token, reads, writes)
        return token

    def full_barrier(self):
        toks = [("E", e, self.cnt[e]) for e in ENGS if self.cnt[e] > 0]
        toks += [("D", qs, n) for qs, n in self.dslot.items() if n > 0]
        for e in ENGS:
            self.barrier_wait(e, toks)

    def barrier_wait(self, engine, tokens):
        waits = self._emit_waits(engine, tokens)
        sem_of = self.sems

        def closure(eng, waits=waits):
            for sname, val in waits:
                eng.wait_ge(sem_of[sname], val)

        self.ops[engine].append(closure)

    def build_and_emit(self, es):
        nc = self.nc
        for e in ENGS:
            self.sems[e] = es.enter_context(nc.semaphore("sem_" + e))
        for q in self.dq:
            for s in range(self.NDSLOT[q]):
                self.sems[f"dma_{q}:{s}"] = es.enter_context(
                    nc.semaphore(f"dsem_{q}_{s}"))
        with nc.Block() as block:
            @block.sync
            def _(eng):
                for f in self.ops["sync"]:
                    f(eng)

            @block.scalar
            def _(eng):
                for f in self.ops["scalar"]:
                    f(eng)

            @block.vector
            def _(eng):
                for f in self.ops["vector"]:
                    f(eng)

            @block.tensor
            def _(eng):
                for f in self.ops["tensor"]:
                    f(eng)

            @block.gpsimd
            def _(eng):
                for f in self.ops["gpsimd"]:
                    f(eng)

    def v_tt(self, out, in0, in1, aluop, reads, writes):
        return self.op("vector", lambda e: e.tensor_tensor(out, in0, in1, aluop),
                       reads, writes)

    def v_stt(self, out, in0, scalar, in1, op0, op1, reads, writes):
        return self.op("vector",
                       lambda e: e.scalar_tensor_tensor(out, in0, scalar, in1, op0, op1),
                       reads, writes)

    def v_ts(self, out, in0, s1, s2, op0, op1, reads, writes):
        if s2 is None:
            return self.op("vector",
                           lambda e: e.tensor_scalar(out, in0, s1, None, op0),
                           reads, writes)
        return self.op("vector",
                       lambda e: e.tensor_scalar(out, in0, s1, s2, op0, op1),
                       reads, writes)

    def v_copy(self, out, in0, reads, writes):
        return self.op("vector", lambda e: e.tensor_copy(out, in0), reads, writes)

    def v_red(self, out, in0, aluop, reads, writes, negate=False):
        return self.op("vector",
                       lambda e: e.tensor_reduce(out, in0, AXX, aluop, negate=negate),
                       reads, writes)

    def v_recip(self, out, in0, reads, writes):
        return self.op("vector", lambda e: e.reciprocal(out, in0), reads, writes)

    def v_scan(self, out, a, b, init, reads, writes):
        return self.op("vector",
                       lambda e: e.tensor_tensor_scan(out, a, b, init, ALU.mult, ALU.add),
                       reads, writes)

    def v_memset(self, out, val, reads, writes):
        return self.op("vector", lambda e: e.memset(out, val), reads, writes)

    def s_act(self, out, in0, func, reads, writes, bias=0.0, scale=1.0):
        return self.op("scalar",
                       lambda e: e.activation(out, in0, func, bias=bias, scale=scale),
                       reads, writes)

    def g_tt(self, out, in0, in1, aluop, reads, writes):
        return self.op("gpsimd", lambda e: e.tensor_tensor(out, in0, in1, aluop),
                       reads, writes)

    def g_copy(self, out, in0, reads, writes):
        return self.op("gpsimd", lambda e: e.tensor_copy(out, in0), reads, writes)

    def mm_group(self, psum_ap, pairs, reads, writes, tag=None):
        if not hasattr(self, "mm_log"):
            self.mm_log = []
            self.mm_total = 0
        if tag is None:
            import traceback
            fr = traceback.extract_stack(limit=2)[0]
            tag = f"{fr.name}:{fr.lineno}"
        self.mm_total += len(pairs)
        self.mm_log.append((self.mm_total, tag))
        def fn(e):
            n = len(pairs)
            inst = None
            for i, (l, r) in enumerate(pairs):
                inst = e.matmul(psum_ap, l, r, start=(i == 0), stop=(i == n - 1))
            return inst
        return self.op("tensor", fn, reads, writes)

    def t_transpose(self, out_psum, in_sb, ident, reads, writes):
        if not hasattr(self, "mm_log"):
            self.mm_log = []
            self.mm_total = 0
        self.mm_total += 1
        self.mm_log.append((self.mm_total, "transpose"))
        return self.op("tensor", lambda e: e.transpose(out_psum, in_sb, ident),
                       reads, writes)



class Model:
    def __init__(self, nc, k: K):
        self.nc = nc
        self.k = k
        self.taps = []
        self.psi = 0

    def init_psum(self, es):
        self.pspool = []
        for i in range(4):
            h = es.enter_context(self.nc.psum_tensor(f"pp{i}", [128, 1024], F32))
            self.pspool.append((h, f"ps{i}"))

    def next_ps(self):
        h, key = self.pspool[self.psi % 4]
        self.psi += 1
        return h, key

    def tap(self, name, src_ap, nelem, dtype=F32):
        if name not in _DEBUG_TAPS:
            return
        h = self.nc.dram_tensor("tap_" + name, [128, nelem], dtype,
                                kind="ExternalOutput")
        keys = [f"tapkey_{name}"]
        self.taps.append(name)
        return h

    def tap_dma(self, name, src_ap, nelem, reads, dtype=F32):
        if name not in _DEBUG_TAPS:
            return
        h = self.nc.dram_tensor("tap_" + name, [128, nelem], dtype,
                                kind="ExternalOutput")
        self.k.dma("sync", h.ap(), src_ap, reads=reads, writes=())
        self.taps.append(name)

    def init_constants(self):
        k = self.k
        nc = self.nc
        self.ident = k.alloc("ident", 128, F32)
        self.oneh = k.alloc("oneh", NE * 128, F32)
        self.ones_col_bf = k.alloc("ones_col_bf", 1, BF16)
        self.ones_row = k.alloc("ones_row", 128, F32)

        k.op("gpsimd",
             lambda e: e.iota(self.ident[:, 0:128], pattern=[[-1, 128]], base=0,
                              channel_multiplier=1,
                              allow_small_or_imprecise_dtypes=True),
             reads=(), writes=("ident",))
        k.v_ts(self.ident[:, 0:128], self.ident[:, 0:128], 0.0, None,
               ALU.is_equal, None, reads=("ident",), writes=("ident",))
        k.op("gpsimd",
             lambda e: e.iota(self.oneh[0:NE, 0:NE * 128], pattern=[[-1, NE], [0, 128]],
                              base=0, channel_multiplier=1,
                              allow_small_or_imprecise_dtypes=True),
             reads=(), writes=("oneh",))
        k.v_ts(self.oneh[0:NE, 0:NE * 128], self.oneh[0:NE, 0:NE * 128], 0.0, None,
               ALU.is_equal, None, reads=("oneh",), writes=("oneh",))
        k.v_memset(self.ones_col_bf[:, 0:1], 1.0, reads=(), writes=("ones_col_bf",))
        k.v_memset(self.ones_row[0:1, 0:128], 1.0, reads=(), writes=("ones_row",))
        self.cb = k.alloc("cbias", 4, F32)
        for i, val in enumerate((EPS1, EPS2, 1.0)):
            k.v_memset(self.cb[:, i:i + 1], val, reads=(), writes=("cbias",))

    def cbias(self, idx):
        return self.cb[:, idx:idx + 1]

    def init_vectors(self, dr):
        k = self.k
        self.vec = {}
        specs = [
            ("norm1_w", DT), ("norm2_w", DT),
            ("left_b", DT), ("right_b", DT), ("out_b", DT), ("conv_b", DT),
            ("down_b", DT), ("up1_b", FT), ("up2_b", FT),
            ("ga_b", DT), ("gx_b", DT), ("a_param", DT),
            ("conv_w", DT * 4),
        ]
        for name, n in specs:
            b = k.alloc("v_" + name, n, F32)
            k.dma("sync", b[:, 0:n], dr[name].ap(), reads=(), writes=("v_" + name,))
            self.vec[name] = b
        gwf = k.alloc("v_gate_w_f", DT * NE, F32)
        k.dma("sync", gwf[:, 0:DT * NE], dr["gate_w"].ap(), reads=(),
              writes=("v_gate_w_f",))
        self.gwb = k.alloc("v_gate_w", DT * NE, BF16)
        k.v_copy(self.gwb[:, 0:DT * NE], gwf[:, 0:DT * NE],
                 reads=("v_gate_w_f",), writes=("v_gate_w",))
        for nm in ("ga_b", "gx_b"):
            b = k.alloc("v_" + nm + "_h", DT, F32)
            k.v_ts(b[:, 0:DT], self.vec[nm][:, 0:DT], 0.5, None, ALU.mult, None,
                   reads=("v_" + nm,), writes=("v_" + nm + "_h",))
            self.vec[nm + "_h"] = b
        ap_ = self.vec["a_param"]
        t1 = k.alloc("v_sp_t1", DT, F32)
        t2 = k.alloc("v_sp_t2", DT, F32)
        m4sp = k.alloc("v_m4sp", DT, F32)
        k.s_act(t1[:, 0:DT], ap_[:, 0:DT], AF.Abs, reads=("v_a_param",),
                writes=("v_sp_t1",))
        k.s_act(t1[:, 0:DT], t1[:, 0:DT], AF.Exp, reads=("v_sp_t1",),
                writes=("v_sp_t1",), scale=-1.0)
        k.s_act(t1[:, 0:DT], t1[:, 0:DT], AF.Ln, reads=("v_sp_t1", "cbias"),
                writes=("v_sp_t1",), bias=self.cb[:, 2:3])
        k.v_ts(t2[:, 0:DT], ap_[:, 0:DT], 0.0, None, ALU.max, None,
               reads=("v_a_param",), writes=("v_sp_t2",))
        k.v_tt(m4sp[:, 0:DT], t1[:, 0:DT], t2[:, 0:DT], ALU.add,
               reads=("v_sp_t1", "v_sp_t2"), writes=("v_m4sp",))
        k.v_ts(m4sp[:, 0:DT], m4sp[:, 0:DT], -4.0, None, ALU.mult, None,
               reads=("v_m4sp",), writes=("v_m4sp",))
        self.m4sp = m4sp

    def wblock_dma(self, dstbuf, dst_off, wdram, k0t, nk, m0, mb, wkey):
        src = wdram.ap().rearrange("(kt p) m -> p kt m", p=128)[
            :, k0t:k0t + nk, m0:m0 + mb]
        dst = dstbuf[:, dst_off:dst_off + nk * mb].rearrange(
            "p (kt m) -> p kt m", kt=nk)
        self.k.dma("gpsimd", dst, src, reads=(), writes=(wkey,))

    def stats_rows(self, xbuf, xkey, want_mean, tag, tile_keys=None):
        k = self.k
        xbf = k.alloc(f"st_xbf_{tag}", DT * TMC, BF16) if want_mean else None
        sq = k.alloc(f"st_sq_{tag}", DT * TMC, BF16)
        ssq_row = k.alloc(f"row_ssq_{tag}", TMC, F32)
        sum_row = k.alloc(f"row_sum_{tag}", TMC, F32) if want_mean else None
        onescol = self.ones_col_bf[:, 0:1]
        for kt in range(DT):
            sl = slice(kt * TMC, (kt + 1) * TMC)
            rk = (tile_keys[kt],) if tile_keys else (xkey,)
            if want_mean:
                k.v_copy(xbf[:, sl], xbuf[:, sl], reads=rk, writes=(xbf.name,))
                k.op("scalar",
                     lambda e, o=sq[:, sl], i=xbf[:, sl]:
                         e.activation(o, i, AF.Square),
                     reads=(xbf.name,), writes=(sq.name,))
            else:
                k.op("scalar",
                     lambda e, o=sq[:, sl], i=xbuf[:, sl]:
                         e.activation(o, i, AF.Square),
                     reads=rk, writes=(sq.name,))
        for n in range(NN):
            if want_mean:
                ps, pk = self.next_ps()
                pairs = [(onescol, xbf[:, kt * TMC + n * 512: kt * TMC + n * 512 + 512])
                         for kt in range(DT)]
                k.mm_group(ps.ap()[0:1, 0:512], pairs,
                           reads=(xbf.name, "ones_col_bf"), writes=(pk,))
                k.v_copy(sum_row[0:1, n * 512:(n + 1) * 512], ps.ap()[0:1, 0:512],
                         reads=(pk,), writes=(sum_row.name,))
            ps, pk = self.next_ps()
            pairs = [(onescol, sq[:, kt * TMC + n * 512: kt * TMC + n * 512 + 512])
                     for kt in range(DT)]
            k.mm_group(ps.ap()[0:1, 0:512], pairs, reads=(sq.name, "ones_col_bf"),
                       writes=(pk,))
            k.v_copy(ssq_row[0:1, n * 512:(n + 1) * 512], ps.ap()[0:1, 0:512],
                     reads=(pk,), writes=(ssq_row.name,))
        if xbf is not None:
            k.free(xbf.name)
        k.free(sq.name)
        return sum_row, ssq_row

    def rstd_row(self, out, outkey, var_ap, varkey, eps_idx, tag):
        k = self.k
        k.s_act(out, var_ap, AF.Ln, reads=(varkey, "cbias"), writes=(outkey,),
                bias=self.cb[0:1, eps_idx:eps_idx + 1])
        k.s_act(out, out, AF.Exp, reads=(outkey,), writes=(outkey,), scale=-0.5)

    def bcast_row(self, rowbuf, rowkey, tag):
        k = self.k
        B = k.alloc(f"bc_{tag}", TMC, F32)
        for n in range(NN):
            ps, pk = self.next_ps()
            k.mm_group(ps.ap()[:, 0:512],
                       [(self.ones_row[0:1, 0:128],
                         rowbuf[0:1, n * 512:(n + 1) * 512])],
                       reads=(rowkey, "ones_row"), writes=(pk,))
            k.v_copy(B[:, n * 512:(n + 1) * 512], ps.ap()[:, 0:512], reads=(pk,),
                     writes=(B.name,))
        return B

    def mm_stream(self, wdram, Mtiles, rhs_of, evac, tag, ring_slots=3,
                  rhs_keys=(), nk=DT, tpb=2):
        k = self.k
        nb = (Mtiles + tpb - 1) // tpb
        ring = []
        for i in range(ring_slots):
            bu = k.alloc(f"wr_{tag}_{i}", nk * 128 * tpb, BF16)
            ring.append(bu)

        def loadblk(bi):
            slot = ring[bi % ring_slots]
            m0 = bi * tpb * 128
            mb = min(tpb * 128, Mtiles * 128 - m0)
            self.wblock_dma(slot, 0, wdram, 0, nk, m0, mb, slot.name)

        for bi in range(min(2, nb)):
            loadblk(bi)
        for bi in range(nb):
            if bi + 2 < nb:
                loadblk(bi + 2)
            slot = ring[bi % ring_slots]
            mhere = min(tpb, Mtiles - bi * tpb)
            bw = mhere * 128
            for mi in range(mhere):
                mt = bi * tpb + mi
                ps, pk = self.next_ps()
                for n in range(NN):
                    pairs = [(slot[:, kt * bw + mi * 128: kt * bw + mi * 128 + 128],
                              rhs_of(kt, n)) for kt in range(nk)]
                    k.mm_group(ps.ap()[:, n * 512:(n + 1) * 512], pairs,
                               reads=(slot.name,) + tuple(rhs_keys), writes=(pk,))
                evac(mt, ps, pk)
        for bu in ring:
            k.free(bu.name)

    def ffn_stage(self, xb, xkey, mc, tag, tile_keys=None):
        k = self.k
        nc = self.nc
        sum_row, ssq_row = self.stats_rows(
            xb, xkey, True, tag,
            tile_keys=tile_keys if getattr(build_core, "TK_STATS", True) else None)
        m_row = k.alloc(f"row_m_{tag}", TMC, F32)
        v_row = k.alloc(f"row_v_{tag}", TMC, F32)
        s_row = k.alloc(f"row_s_{tag}", TMC, F32)
        rT = slice(0, TMC)
        k.v_ts(m_row[0:1, rT], sum_row[0:1, rT], 1.0 / D, None, ALU.mult, None,
               reads=(sum_row.name,), writes=(m_row.name,))
        k.op("vector", lambda e: e.tensor_tensor(v_row[0:1, rT], m_row[0:1, rT],
                                                 m_row[0:1, rT], ALU.mult),
             reads=(m_row.name,), writes=(v_row.name,))
        k.v_stt(v_row[0:1, rT], ssq_row[0:1, rT], 1.0 / D, v_row[0:1, rT],
                ALU.mult, ALU.subtract, reads=(ssq_row.name, v_row.name),
                writes=(v_row.name,))
        self.rstd_row(s_row[0:1, rT], s_row.name, v_row[0:1, rT], v_row.name,
                      0, tag)
        mB = self.bcast_row(m_row, m_row.name, f"m_{tag}")
        sB = self.bcast_row(s_row, s_row.name, f"s_{tag}")
        k.free(sum_row.name); k.free(ssq_row.name)
        k.free(m_row.name); k.free(v_row.name); k.free(s_row.name)
        u = k.alloc(f"u_{tag}", DT * TMC, BF16)
        tn = k.alloc(f"un_{tag}", DT * TMC, F32)
        x3 = xb[:, 0:DT * TMC].rearrange("p (kt t) -> p kt t", kt=DT)
        t3 = tn[:, 0:DT * TMC].rearrange("p (kt t) -> p kt t", kt=DT)
        u3 = u[:, 0:DT * TMC].rearrange("p (kt t) -> p kt t", kt=DT)
        mB3 = mB[:, 0:TMC].unsqueeze(1).broadcast_to([128, DT, TMC])
        sB3 = sB[:, 0:TMC].unsqueeze(1).broadcast_to([128, DT, TMC])
        k.v_tt(t3, x3, mB3, ALU.subtract,
               reads=(xkey,) + (tuple(tile_keys) if tile_keys else ()) +
                     (mB.name,), writes=(tn.name,))
        k.v_tt(u3, t3, sB3, ALU.mult, reads=(tn.name, sB.name), writes=(u.name,))
        k.free(tn.name)
        k.free(mB.name); k.free(sB.name)
        self.tap_dma(f"u_{tag}", u[:, 0:DT * TMC], DT * TMC, (u.name,), BF16)

        combT = k.alloc(f"combT_{tag}", TMC, F32)
        sm = k.alloc(f"sm_{tag}", 8, F32)
        sc = k.alloc(f"sc_{tag}", 8, F32)
        r1 = k.alloc(f"r1_{tag}", 8, F32)
        ct = k.alloc(f"ct_{tag}", 8, F32)
        red = k.alloc(f"red_{tag}", 4, F32)
        for tt in range(TMC // 128):
            ps, pk = self.next_ps()
            pairs = [(u[:, kt * TMC + tt * 128: kt * TMC + tt * 128 + 128],
                      self.gwb[:, kt * NE:(kt + 1) * NE]) for kt in range(DT)]
            k.mm_group(ps.ap()[:, 0:NE], pairs, reads=(u.name, "v_gate_w"),
                       writes=(pk,))
            k.v_red(red[:, 0:1], ps.ap()[:, 0:NE], ALU.max, reads=(pk,),
                    writes=(red.name,), negate=True)
            k.op("scalar", lambda e, o=sm[:, 0:NE], i=ps.ap()[:, 0:NE],
                 b=red[:, 0:1]: e.activation(o, i, AF.Exp, bias=b),
                 reads=(pk, red.name), writes=(sm.name,))
            k.v_red(red[:, 1:2], sm[:, 0:NE], ALU.add, reads=(sm.name,),
                    writes=(red.name,))
            k.v_recip(red[:, 2:3], red[:, 1:2], reads=(red.name,), writes=(red.name,))
            k.v_ts(sc[:, 0:NE], sm[:, 0:NE], red[:, 2:3], None, ALU.mult, None,
                   reads=(sm.name, red.name), writes=(sc.name,))
            k.v_red(red[:, 0:1], sc[:, 0:NE], ALU.max, reads=(sc.name,),
                    writes=(red.name,))
            k.v_ts(ct[:, 0:NE], sc[:, 0:NE], red[:, 0:1], None, ALU.is_ge, None,
                   reads=(sc.name, red.name), writes=(ct.name,))
            k.v_stt(ct[:, 0:NE], ct[:, 0:NE], -1e30, sc[:, 0:NE], ALU.mult, ALU.add,
                    reads=(ct.name, sc.name), writes=(ct.name,))
            k.v_red(red[:, 0:1], ct[:, 0:NE], ALU.max, reads=(ct.name,),
                    writes=(red.name,))
            k.v_ts(ct[:, 0:NE], sc[:, 0:NE], red[:, 0:1], None, ALU.is_ge, None,
                   reads=(sc.name, red.name), writes=(ct.name,))
            k.v_stt(ct[:, 0:NE], sc[:, 0:NE], 0.5, ct[:, 0:NE], ALU.mult, ALU.mult,
                    reads=(sc.name, ct.name), writes=(ct.name,))
            ps2, pk2 = self.next_ps()
            k.t_transpose(ps2.ap()[0:NE, 0:128], ct[:, 0:NE], self.ident[:, 0:128],
                          reads=(ct.name, "ident"), writes=(pk2,))
            k.v_copy(combT[0:NE, tt * 128:(tt + 1) * 128], ps2.ap()[0:NE, 0:128],
                     reads=(pk2,), writes=(combT.name,))
        for b in (sm, sc, r1, ct, red):
            k.free(b.name)
        self.tap_dma(f"combT_{tag}", combT[0:NE, 0:TMC], TMC, (combT.name,))

        combB = k.alloc(f"combB_{tag}", TMC, F32)
        gs = k.alloc(f"gs_{tag}", HT * TMC, BF16)
        silu_t = k.alloc(f"silu_{tag}", TMC, F32)
        wdq = [k.alloc(f"wd_{tag}_{i}", HT * 512, BF16) for i in range(2)]
        wgu_ring = [k.alloc(f"wgu_{tag}_{i}", DT * 256, BF16) for i in range(3)]
        dr = self.dr

        def load_wgu(e, ht):
            slot = wgu_ring[ht % 3]
            self.wblock_dma(slot, 0, dr["wgu"], e * DT, DT, ht * 256, 256,
                            slot.name)

        def load_wdq(e, q):
            buf = wdq[q % 2]
            srcv = dr["w_down"].ap().rearrange("(kt p) m -> p kt m", p=128)[
                :, e * HT:(e + 1) * HT, q * 512:(q + 1) * 512]
            dstv = buf[:, 0:HT * 512].rearrange("p (kt m) -> p kt m", kt=HT)
            k.dma("gpsimd", dstv, srcv, reads=(), writes=(buf.name,))

        for e in range(NE):
            for n in range(NN):
                ps, pk = self.next_ps()
                k.mm_group(ps.ap()[:, n * 512:(n + 1) * 512],
                           [(self.oneh[0:NE, e * 128:(e + 1) * 128],
                             combT[0:NE, n * 512:(n + 1) * 512])],
                           reads=(combT.name, "oneh"), writes=(pk,))
                k.s_act(combB[:, n * 512:(n + 1) * 512],
                        ps.ap()[:, n * 512:(n + 1) * 512], AF.Identity,
                        reads=(pk,), writes=(combB.name,))
            load_wgu(e, 0); load_wgu(e, 1)
            load_wdq(e, 0)
            for ht in range(HT):
                if ht + 2 < HT:
                    load_wgu(e, ht + 2)
                slot = wgu_ring[ht % 3]
                psg, pkg = self.next_ps()
                psu, pku = self.next_ps()
                for n in range(NN):
                    pairs = [(slot[:, kt * 256: kt * 256 + 128],
                              u[:, kt * TMC + n * 512: kt * TMC + n * 512 + 512])
                             for kt in range(DT)]
                    k.mm_group(psg.ap()[:, n * 512:(n + 1) * 512], pairs,
                               reads=(slot.name, u.name), writes=(pkg,))
                    pairs = [(slot[:, kt * 256 + 128: kt * 256 + 256],
                              u[:, kt * TMC + n * 512: kt * TMC + n * 512 + 512])
                             for kt in range(DT)]
                    k.mm_group(psu.ap()[:, n * 512:(n + 1) * 512], pairs,
                               reads=(slot.name, u.name), writes=(pku,))
                k.s_act(silu_t[:, 0:TMC], psg.ap()[:, 0:TMC], AF.Silu,
                        reads=(pkg,), writes=(silu_t.name,))
                k.v_tt(silu_t[:, 0:TMC], silu_t[:, 0:TMC], combB[:, 0:TMC],
                       ALU.mult, reads=(silu_t.name, combB.name),
                       writes=(silu_t.name,))
                k.v_tt(gs[:, ht * TMC:(ht + 1) * TMC], silu_t[:, 0:TMC],
                       psu.ap()[:, 0:TMC], ALU.mult,
                       reads=(silu_t.name, pku), writes=(gs.name,))
            for q in range(4):
                if q + 1 < 4:
                    load_wdq(e, q + 1)
                buf = wdq[q % 2]
                for mi in range(4):
                    mt = q * 4 + mi
                    ps, pk = self.next_ps()
                    for n in range(NN):
                        pairs = [(buf[:, kt * 512 + mi * 128: kt * 512 + mi * 128 + 128],
                                  gs[:, kt * TMC + n * 512: kt * TMC + n * 512 + 512])
                                 for kt in range(HT)]
                        k.mm_group(ps.ap()[:, n * 512:(n + 1) * 512], pairs,
                                   reads=(buf.name, gs.name), writes=(pk,))
                    sl = slice(mt * TMC, (mt + 1) * TMC)
                    xrk = (tile_keys[mt],) if (tile_keys and e == 0) else ()
                    k.v_tt(xb[:, sl], xb[:, sl], ps.ap()[:, 0:TMC], ALU.add,
                           reads=(xkey, pk) + xrk, writes=(xkey,))
        for b in wgu_ring:
            k.free(b.name)
        for b in wdq:
            k.free(b.name)
        k.free(gs.name); k.free(silu_t.name)
        k.free(combB.name); k.free(combT.name)

        sg_buf = k.alloc(f"sg_{tag}", HST * TMC, BF16)
        silu_t2 = k.alloc(f"silu2_{tag}", TMC, F32)
        swg_ring = [k.alloc(f"swg_{tag}_{i}", DT * 256, BF16) for i in range(3)]

        def load_swgu(mt):
            slot = swg_ring[mt % 3]
            self.wblock_dma(slot, 0, dr["swgu"], 0, DT, mt * 256, 256, slot.name)

        load_swgu(0); load_swgu(1)
        for mt in range(HST):
            if mt + 2 < HST:
                load_swgu(mt + 2)
            slot = swg_ring[mt % 3]
            psg, pkg = self.next_ps()
            psu, pku = self.next_ps()
            for n in range(NN):
                pairs = [(slot[:, kt * 256: kt * 256 + 128],
                          u[:, kt * TMC + n * 512: kt * TMC + n * 512 + 512])
                         for kt in range(DT)]
                k.mm_group(psg.ap()[:, n * 512:(n + 1) * 512], pairs,
                           reads=(slot.name, u.name), writes=(pkg,))
                pairs = [(slot[:, kt * 256 + 128: kt * 256 + 256],
                          u[:, kt * TMC + n * 512: kt * TMC + n * 512 + 512])
                         for kt in range(DT)]
                k.mm_group(psu.ap()[:, n * 512:(n + 1) * 512], pairs,
                           reads=(slot.name, u.name), writes=(pku,))
            k.s_act(silu_t2[:, 0:TMC], psg.ap()[:, 0:TMC], AF.Silu,
                    reads=(pkg,), writes=(silu_t2.name,))
            k.v_tt(sg_buf[:, mt * TMC:(mt + 1) * TMC], silu_t2[:, 0:TMC],
                   psu.ap()[:, 0:TMC], ALU.mult,
                   reads=(silu_t2.name, pku), writes=(sg_buf.name,))
        for b in swg_ring:
            k.free(b.name)
        k.free(silu_t2.name)
        swd2 = [k.alloc(f"swd_{tag}_{i}", HST * 256, BF16) for i in range(2)]

        def load_swd(dh):
            self.wblock_dma(swd2[dh % 2], 0, dr["sw_down"], 0, 11, dh * 256,
                            256, swd2[dh % 2].name)
            self.wblock_dma(swd2[dh % 2], 11 * 256, dr["sw_down"], 11, 11,
                            dh * 256, 256, swd2[dh % 2].name)

        load_swd(0)
        for dh in range(8):
            if dh + 1 < 8:
                load_swd(dh + 1)
            swd = swd2[dh % 2]
            for mt8 in range(2):
                mt = dh * 2 + mt8
                ps, pk = self.next_ps()
                for n in range(NN):
                    pairs = [(swd[:, kt * 256 + mt8 * 128: kt * 256 + mt8 * 128 + 128],
                              sg_buf[:, kt * TMC + n * 512: kt * TMC + n * 512 + 512])
                             for kt in range(HST)]
                    k.mm_group(ps.ap()[:, n * 512:(n + 1) * 512], pairs,
                               reads=(swd.name, sg_buf.name), writes=(pk,))
                sl = slice(mt * TMC, (mt + 1) * TMC)
                k.v_stt(xb[:, sl], ps.ap()[:, 0:TMC], 0.5, xb[:, sl],
                        ALU.mult, ALU.add, reads=(pk, xkey),
                        writes=(xkey, f"{xkey}.{tag}.t{mt}"))
        for bu in swd2:
            k.free(bu.name)
        k.free(sg_buf.name)
        k.free(u.name)
        return [f"{xkey}.{tag}.t{mt}" for mt in range(DT)]

    def attn_stage(self, xb, xkey, mc, tag, tile_keys=None):
        k = self.k
        dr = self.dr
        vec = self.vec
        rT = slice(0, TMC)
        sum_row, ssq_row = self.stats_rows(
            xb, xkey, True, tag,
            tile_keys=tile_keys if getattr(build_core, "TK_STATS", True) else None)
        m_row = k.alloc(f"row_m_{tag}", TMC, F32)
        v_row = k.alloc(f"row_v_{tag}", TMC, F32)
        s1_row = k.alloc(f"row_s1_{tag}", TMC, F32)
        sc_row = k.alloc(f"row_sc_{tag}", TMC, F32)
        t_row = k.alloc(f"row_t_{tag}", TMC, F32)
        k.v_ts(m_row[0:1, rT], sum_row[0:1, rT], 1.0 / D, None, ALU.mult, None,
               reads=(sum_row.name,), writes=(m_row.name,))
        k.v_tt(v_row[0:1, rT], m_row[0:1, rT], m_row[0:1, rT], ALU.mult,
               reads=(m_row.name,), writes=(v_row.name,))
        k.v_stt(v_row[0:1, rT], ssq_row[0:1, rT], 1.0 / D, v_row[0:1, rT],
                ALU.mult, ALU.subtract, reads=(ssq_row.name, v_row.name),
                writes=(v_row.name,))
        self.rstd_row(s1_row[0:1, rT], s1_row.name, v_row[0:1, rT], v_row.name,
                      0, tag)
        k.v_ts(t_row[0:1, rT], v_row[0:1, rT], EPS1, None, ALU.add, None,
               reads=(v_row.name,), writes=(t_row.name,))
        k.v_recip(t_row[0:1, rT], t_row[0:1, rT], reads=(t_row.name,),
                  writes=(t_row.name,))
        k.v_tt(t_row[0:1, rT], v_row[0:1, rT], t_row[0:1, rT], ALU.mult,
               reads=(v_row.name, t_row.name), writes=(t_row.name,))
        self.rstd_row(sc_row[0:1, rT], sc_row.name, t_row[0:1, rT], t_row.name,
                      1, tag)
        k.v_tt(sc_row[0:1, rT], sc_row[0:1, rT], s1_row[0:1, rT], ALU.mult,
               reads=(sc_row.name, s1_row.name), writes=(sc_row.name,))
        m1B = self.bcast_row(m_row, m_row.name, f"m1_{tag}")
        s1B = self.bcast_row(s1_row, s1_row.name, f"s1_{tag}")
        scB = self.bcast_row(sc_row, sc_row.name, f"sc_{tag}")
        for b in (sum_row, ssq_row, m_row, v_row, s1_row, sc_row, t_row):
            k.free(b.name)
        xn = k.alloc(f"xn_{tag}", DT * TMC, BF16)
        tn = k.alloc(f"tn_{tag}", DT * TMC, F32)
        x3 = xb[:, 0:DT * TMC].rearrange("p (kt t) -> p kt t", kt=DT)
        t3 = tn[:, 0:DT * TMC].rearrange("p (kt t) -> p kt t", kt=DT)
        m1B3 = m1B[:, 0:TMC].unsqueeze(1).broadcast_to([128, DT, TMC])
        k.v_tt(t3, x3, m1B3, ALU.subtract,
               reads=(xkey,) + (tuple(tile_keys) if tile_keys else ()) +
                     (m1B.name,), writes=(tn.name,))
        for kt in range(DT):
            sl = slice(kt * TMC, (kt + 1) * TMC)
            k.v_stt(xn[:, sl], tn[:, sl], vec["norm1_w"][:, kt:kt + 1],
                    scB[:, 0:TMC], ALU.mult, ALU.mult,
                    reads=(tn.name, scB.name, "v_norm1_w"), writes=(xn.name,))
        k.free(tn.name)
        k.free(scB.name)
        self.tap_dma(f"xn_{tag}", xn[:, 0:DT * TMC], DT * TMC, (xn.name,), BF16)
        xsp3 = self.xsp.ap().rearrange("p (kt t) -> p kt t", kt=DT)
        xb3 = xb[:, 0:DT * TMC].rearrange("p (kt t) -> p kt t", kt=DT)
        k.dma("sync", xsp3, xb3,
              reads=(xkey,) + (tuple(tile_keys) if tile_keys else ()),
              writes=("xspill",))
        k.free(xb.name)

        vbuf = k.alloc(f"v_{tag}", DT * TMC, F32)
        left = k.alloc(f"left_{tag}", DT * TMC, BF16)
        cbf = k.alloc(f"cbf_{tag}", DT * TMC, BF16)
        ctmp = k.alloc(f"ctmp_{tag}", TMC, F32)
        cw = vec["conv_w"]
        cb = vec["conv_b"]
        halo = self.vt0 if mc == 0 else self.vt1
        halokey = "vt0" if mc == 0 else "vt1"
        rl_ring = [k.alloc(f"rl_{tag}_{i}", DT * 256, BF16) for i in range(3)]

        def load_rl(mt):
            slot = rl_ring[mt % 3]
            self.wblock_dma(slot, 0, dr["rlW"], 0, DT, mt * 256, 256, slot.name)

        def rhs_xn(kt, n):
            return xn[:, kt * TMC + n * 512: kt * TMC + n * 512 + 512]

        load_rl(0); load_rl(1)
        for mt in range(DT):
            if mt + 2 < DT:
                load_rl(mt + 2)
            slot = rl_ring[mt % 3]
            sl = slice(mt * TMC, (mt + 1) * TMC)
            psv, pkv = self.next_ps()
            psl, pkl = self.next_ps()
            for n in range(NN):
                pairs = [(slot[:, kt * 256: kt * 256 + 128], rhs_xn(kt, n))
                         for kt in range(DT)]
                k.mm_group(psv.ap()[:, n * 512:(n + 1) * 512], pairs,
                           reads=(slot.name, xn.name), writes=(pkv,))
                pairs = [(slot[:, kt * 256 + 128: kt * 256 + 256], rhs_xn(kt, n))
                         for kt in range(DT)]
                k.mm_group(psl.ap()[:, n * 512:(n + 1) * 512], pairs,
                           reads=(slot.name, xn.name), writes=(pkl,))
            k.s_act(vbuf[:, sl], psv.ap()[:, 0:TMC], AF.Identity,
                    reads=(pkv, "v_right_b"), writes=(vbuf.name,),
                    bias=vec["right_b"][:, mt:mt + 1])
            k.s_act(left[:, sl], psl.ap()[:, 0:TMC], AF.Gelu,
                    reads=(pkl, "v_left_b"), writes=(left.name,),
                    bias=vec["left_b"][:, mt:mt + 1])
            if mc == 0:
                k.v_copy(self.vt1[:, mt * 3:mt * 3 + 3],
                         vbuf[:, mt * TMC + TMC - 3: mt * TMC + TMC],
                         reads=(vbuf.name,), writes=("vt1",))
            k.v_ts(ctmp[:, 0:TMC], vbuf[:, sl], cw[:, mt * 4 + 3:mt * 4 + 4],
                   cb[:, mt:mt + 1], ALU.mult, ALU.add,
                   reads=(vbuf.name, "v_conv_w", "v_conv_b"), writes=(ctmp.name,))
            for s in range(1, 4):
                w_ap = cw[:, mt * 4 + 3 - s:mt * 4 + 4 - s]
                k.v_stt(ctmp[:, s:TMC], vbuf[:, mt * TMC: mt * TMC + TMC - s],
                        w_ap, ctmp[:, s:TMC], ALU.mult, ALU.add,
                        reads=(vbuf.name, "v_conv_w", ctmp.name),
                        writes=(ctmp.name,))
                k.v_stt(ctmp[:, 0:s], halo[:, mt * 3 + 3 - s: mt * 3 + 3],
                        w_ap, ctmp[:, 0:s], ALU.mult, ALU.add,
                        reads=(halokey, "v_conv_w", ctmp.name),
                        writes=(ctmp.name,))
            k.v_copy(vbuf[:, sl], ctmp[:, 0:TMC], reads=(ctmp.name,),
                     writes=(vbuf.name,))
            k.v_copy(cbf[:, sl], vbuf[:, sl], reads=(vbuf.name,),
                     writes=(cbf.name,))
        for bu in rl_ring:
            k.free(bu.name)
        k.free(ctmp.name)
        k.free(xn.name)
        cbuf = vbuf
        self.tap_dma(f"c_{tag}", cbuf[:, 0:DT * TMC], DT * TMC, (cbuf.name,))

        lr_all = cbuf.ap().bitcast(BF16)
        A = k.alloc(f"gA_{tag}", TMC, F32)
        Bv = k.alloc(f"gB_{tag}", TMC, F32)
        Cv = k.alloc(f"gC_{tag}", TMC, F32)
        Dv = k.alloc(f"gD_{tag}", TMC, F32)
        Ev = k.alloc(f"gE_{tag}", TMC, F32)
        gring = [k.alloc(f"gw_{tag}_{i}", DT * 256, BF16) for i in range(3)]

        def rhs_c(kt, n):
            return cbf[:, kt * TMC + n * 512: kt * TMC + n * 512 + 512]

        def load_gagx(mt):
            slot = gring[mt % 3]
            self.wblock_dma(slot, 0, dr["gagx"], 0, DT, mt * 256, 256, slot.name)

        load_gagx(0); load_gagx(1)
        for mt in range(DT):
            if mt + 2 < DT:
                load_gagx(mt + 2)
            slot = gring[mt % 3]
            psa, pka = self.next_ps()
            psb, pkb = self.next_ps()
            for n in range(NN):
                pairs = [(slot[:, kt * 256: kt * 256 + 128], rhs_c(kt, n))
                         for kt in range(DT)]
                k.mm_group(psa.ap()[:, n * 512:(n + 1) * 512], pairs,
                           reads=(slot.name, cbf.name), writes=(pka,))
                pairs = [(slot[:, kt * 256 + 128: kt * 256 + 256], rhs_c(kt, n))
                         for kt in range(DT)]
                k.mm_group(psb.ap()[:, n * 512:(n + 1) * 512], pairs,
                           reads=(slot.name, cbf.name), writes=(pkb,))
            k.s_act(A[:, 0:TMC], psa.ap()[:, 0:TMC], AF.Tanh,
                    reads=(pka, "v_ga_b_h"), writes=(A.name,),
                    bias=vec["ga_b_h"][:, mt:mt + 1], scale=0.5)
            k.s_act(Bv[:, 0:TMC], psb.ap()[:, 0:TMC], AF.Tanh,
                    reads=(pkb, "v_gx_b_h"), writes=(Bv.name,),
                    bias=vec["gx_b_h"][:, mt:mt + 1], scale=0.5)
            k.v_ts(A[:, 0:TMC], A[:, 0:TMC], self.m4sp[:, mt:mt + 1],
                   self.m4sp[:, mt:mt + 1], ALU.mult, ALU.add,
                   reads=(A.name, "v_m4sp"), writes=(A.name,))
            k.s_act(Cv[:, 0:TMC], A[:, 0:TMC], AF.Exp, reads=(A.name,),
                    writes=(Cv.name,))
            k.s_act(A[:, 0:TMC], Cv[:, 0:TMC], AF.Square, reads=(Cv.name,),
                    writes=(A.name,))
            k.v_ts(A[:, 0:TMC], A[:, 0:TMC], -1.0, 1.0, ALU.mult, ALU.add,
                   reads=(A.name,), writes=(A.name,))
            k.v_ts(A[:, 0:TMC], A[:, 0:TMC], 0.0, None, ALU.max, None,
                   reads=(A.name,), writes=(A.name,))
            k.s_act(A[:, 0:TMC], A[:, 0:TMC], AF.Ln, reads=(A.name,),
                    writes=(A.name,))
            k.s_act(A[:, 0:TMC], A[:, 0:TMC], AF.Exp, reads=(A.name,),
                    writes=(A.name,), scale=0.5)
            k.v_ts(Bv[:, 0:TMC], Bv[:, 0:TMC], 0.5, 0.5, ALU.mult, ALU.add,
                   reads=(Bv.name,), writes=(Bv.name,))
            k.v_tt(Dv[:, 0:TMC], Bv[:, 0:TMC],
                   cbuf[:, mt * TMC:(mt + 1) * TMC], ALU.mult,
                   reads=(Bv.name, cbuf.name), writes=(Dv.name,))
            k.v_tt(Dv[:, 0:TMC], A[:, 0:TMC], Dv[:, 0:TMC], ALU.mult,
                   reads=(A.name, Dv.name), writes=(Dv.name,))
            k.v_scan(Ev[:, 0:TMC], Cv[:, 0:TMC], Dv[:, 0:TMC],
                     self.hcar[:, mt:mt + 1], reads=(Cv.name, Dv.name, "hcar"),
                     writes=(Ev.name,))
            k.v_copy(self.hcar[:, mt:mt + 1], Ev[:, TMC - 1:TMC],
                     reads=(Ev.name,), writes=("hcar",))
            lr_mt = lr_all[:, mt * 2 * TMC: mt * 2 * TMC + TMC]
            k.v_tt(lr_mt, left[:, mt * TMC:(mt + 1) * TMC], Ev[:, 0:TMC],
                   ALU.mult, reads=(left.name, Ev.name), writes=("lr",))
            if mt == DT - 1:
                self.tap_dma(f"ga_{tag}", Cv[:, 0:TMC], TMC, (Cv.name,))
                self.tap_dma(f"gb_{tag}", Dv[:, 0:TMC], TMC, (Dv.name,))
                self.tap_dma(f"gh_{tag}", Ev[:, 0:TMC], TMC, (Ev.name,))
                self.tap_dma(f"gl_{tag}", left[:, mt * TMC:(mt + 1) * TMC], TMC,
                             (left.name,), BF16)
        for b in (A, Bv, Cv, Dv, Ev):
            k.free(b.name)
        for b in gring:
            k.free(b.name)
        k.free(cbf.name)
        k.free(left.name)

        if getattr(build_core, "DBG_BARRIER", "") == "preow":
            k.full_barrier()
        x1 = k.alloc(f"x1_{tag}", DT * TMC, F32)

        def rhs_lr(kt, n):
            return lr_all[:, kt * 2 * TMC + n * 512: kt * 2 * TMC + n * 512 + 512]

        def evac_o(mt, ps, pk):
            k.s_act(x1[:, mt * TMC:(mt + 1) * TMC], ps.ap()[:, 0:TMC],
                    AF.Identity, reads=(pk, "v_out_b"),
                    writes=(x1.name, f"{x1.name}.e{mt}"),
                    bias=vec["out_b"][:, mt:mt + 1])

        self.mm_stream(dr["out_W"], DT, rhs_lr, evac_o, f"ow_{tag}",
                       ring_slots=4, rhs_keys=("lr",))
        k.alias_key(cbuf.name, "lr")
        k.free(cbuf.name)
        if getattr(build_core, "DBG_BARRIER", "") == "postow":
            k.full_barrier()

        xres = k.alloc(f"xres_{tag}", DT * TMC, F32, high=True)
        xsp3 = self.xsp.ap().rearrange("p (kt t) -> p kt t", kt=DT)
        xres3 = xres[:, 0:DT * TMC].rearrange("p (kt t) -> p kt t", kt=DT)
        for q in range(4):
            k.dma("sync", xres3[:, 4 * q:4 * q + 4, :],
                  xsp3[:, 4 * q:4 * q + 4, :],
                  reads=("xspill",), writes=(xres.name, f"{xres.name}.q{q}"))
        self.tap_dma(f"x1pre_{tag}", x1[:, 0:DT * TMC], DT * TMC, (x1.name,))
        self.tap_dma(f"xres0_{tag}", xres[:, 0:DT * TMC], DT * TMC,
                     (xres.name, f"{xres.name}.q0", f"{xres.name}.q1",
                      f"{xres.name}.q2", f"{xres.name}.q3"))
        tn2 = k.alloc(f"tn2_{tag}", TMC, F32)
        tn2b = k.alloc(f"tn2b_{tag}", TMC, F32)
        x1tk = []
        xrtk = []
        for kt in range(DT):
            sl = slice(kt * TMC, (kt + 1) * TMC)
            qk = f"{xres.name}.q{kt // 4}"
            ek = f"{x1.name}.e{kt}"
            tkey = f"{x1.name}.t{kt}"
            rkey = f"{xres.name}.t{kt}"
            x1tk.append(tkey)
            xrtk.append(rkey)
            t_ = tn2
            opf = k.v_tt
            opf(t_[:, 0:TMC], xres[:, sl], m1B[:, 0:TMC], ALU.subtract,
                reads=(qk, m1B.name), writes=(t_.name,))
            opf(t_[:, 0:TMC], t_[:, 0:TMC], s1B[:, 0:TMC], ALU.mult,
                reads=(t_.name, s1B.name), writes=(t_.name,))
            opf(x1[:, sl], x1[:, sl], t_[:, 0:TMC], ALU.add,
                reads=(ek, t_.name), writes=(tkey,))
            opf(xres[:, sl], xres[:, sl], x1[:, sl], ALU.add,
                reads=(qk, tkey), writes=(rkey,))
        k.free(tn2.name)
        k.free(tn2b.name)
        k.free(m1B.name)
        k.free(s1B.name)
        self.tap_dma(f"x1_{tag}", x1[:, 0:DT * TMC], DT * TMC, (x1.name,))

        _, ssq2 = self.stats_rows(x1, x1.name, False, tag + "r2",
                                  tile_keys=x1tk)
        s2_row = k.alloc(f"row_s2_{tag}", TMC, F32)
        k.v_ts(s2_row[0:1, rT], ssq2[0:1, rT], 1.0 / D, None, ALU.mult, None,
               reads=(ssq2.name,), writes=(s2_row.name,))
        self.rstd_row(s2_row[0:1, rT], s2_row.name, s2_row[0:1, rT], s2_row.name,
                      1, tag)
        s2B = self.bcast_row(s2_row, s2_row.name, f"s2_{tag}")
        k.free(ssq2.name); k.free(s2_row.name)
        x1n = k.alloc(f"x1n_{tag}", DT * TMC, BF16)
        tn3 = k.alloc(f"tn3_{tag}", TMC, F32)
        for kt in range(DT):
            sl = slice(kt * TMC, (kt + 1) * TMC)
            k.v_tt(tn3[:, 0:TMC], x1[:, sl], s2B[:, 0:TMC], ALU.mult,
                   reads=(x1tk[kt], s2B.name), writes=(tn3.name,))
            k.v_ts(x1n[:, sl], tn3[:, 0:TMC], vec["norm2_w"][:, kt:kt + 1],
                   None, ALU.mult, None, reads=(tn3.name, "v_norm2_w"),
                   writes=(x1n.name,))
        k.free(tn3.name); k.free(s2B.name)
        for tk in x1tk:
            k.alias_key(x1.name, tk)
        k.free(x1.name)

        gq = k.alloc(f"gq_{tag}", FQT * TMC, BF16)
        dslab = k.alloc(f"dslab_{tag}", FQT * D, BF16)
        tgel = k.alloc(f"tgel_{tag}", TMC, F32)
        uring = [k.alloc(f"ur_{tag}_{i}", DT * 256, BF16) for i in range(3)]

        def rhs_x1n(kt, n):
            return x1n[:, kt * TMC + n * 512: kt * TMC + n * 512 + 512]

        def load_up12(ft):
            slot = uring[ft % 3]
            self.wblock_dma(slot, 0, dr["up12"], 0, DT, ft * 256, 256, slot.name)

        def load_dslab(fq):
            for mh in range(2):
                srcv = dr["down_W"].ap().rearrange("(kt p) m -> p kt m", p=128)[
                    :, fq * FQT:(fq + 1) * FQT, mh * 1024:(mh + 1) * 1024]
                dstv = dslab[:, 0:FQT * D].rearrange("p (kt m) -> p kt m",
                                                     kt=FQT)[
                    :, :, mh * 1024:(mh + 1) * 1024]
                k.dma("gpsimd", dstv, srcv, reads=(), writes=(dslab.name,))

        for fq in range(FQ):
            load_up12(fq * FQT); load_up12(fq * FQT + 1)
            load_dslab(fq)
            for ml in range(FQT):
                ft = fq * FQT + ml
                if ml + 2 < FQT:
                    load_up12(ft + 2)
                slot = uring[ft % 3]
                ps1, pk1 = self.next_ps()
                ps2, pk2 = self.next_ps()
                for n in range(NN):
                    pairs = [(slot[:, kt * 256: kt * 256 + 128], rhs_x1n(kt, n))
                             for kt in range(DT)]
                    k.mm_group(ps1.ap()[:, n * 512:(n + 1) * 512], pairs,
                               reads=(slot.name, x1n.name), writes=(pk1,))
                    pairs = [(slot[:, kt * 256 + 128: kt * 256 + 256],
                              rhs_x1n(kt, n)) for kt in range(DT)]
                    k.mm_group(ps2.ap()[:, n * 512:(n + 1) * 512], pairs,
                               reads=(slot.name, x1n.name), writes=(pk2,))
                k.s_act(tgel[:, 0:TMC], ps1.ap()[:, 0:TMC], AF.Gelu,
                        reads=(pk1, "v_up1_b"), writes=(tgel.name,),
                        bias=vec["up1_b"][:, ft:ft + 1])
                k.v_stt(gq[:, ml * TMC:(ml + 1) * TMC], ps2.ap()[:, 0:TMC],
                        vec["up2_b"][:, ft:ft + 1], tgel[:, 0:TMC],
                        ALU.add, ALU.mult,
                        reads=(pk2, "v_up2_b", tgel.name), writes=(gq.name,))
            for mt in range(DT):
                ps, pk = self.next_ps()
                for n in range(NN):
                    pairs = [(dslab[:, kt * D + mt * 128: kt * D + mt * 128 + 128],
                              gq[:, kt * TMC + n * 512: kt * TMC + n * 512 + 512])
                             for kt in range(FQT)]
                    k.mm_group(ps.ap()[:, n * 512:(n + 1) * 512], pairs,
                               reads=(dslab.name, gq.name), writes=(pk,))
                sl = slice(mt * TMC, (mt + 1) * TMC)
                rkey = f"{xres.name}.t{mt}"
                if fq == 0:
                    k.v_stt(xres[:, sl], ps.ap()[:, 0:TMC],
                            vec["down_b"][:, mt:mt + 1], xres[:, sl],
                            ALU.add, ALU.add,
                            reads=(pk, "v_down_b", rkey), writes=(rkey,))
                else:
                    k.v_tt(xres[:, sl], xres[:, sl], ps.ap()[:, 0:TMC], ALU.add,
                           reads=(rkey, pk), writes=(rkey,))
        for b in uring:
            k.free(b.name)
        k.free(gq.name); k.free(dslab.name); k.free(tgel.name)
        k.free(x1n.name)
        return xres, xrtk


def build_core():
    from contextlib import ExitStack
    nc = bass.Bass("TRN2", target_bir_lowering=False)
    es = ExitStack()
    k = K(nc)
    m = Model(nc, k)
    m.init_psum(es)

    dr = {}
    dr["xin"] = nc.dram_tensor("xin", [128, DT * T], F32, kind="ExternalInput")
    for nm, shp in [("rlW", [D, 2 * D]),
                    ("gagx", [D, 2 * D]), ("out_W", [D, D]),
                    ("up12", [D, 2 * FF]), ("down_W", [FF, D]),
                    ("wgu", [NE * D, 2 * HE]),
                    ("w_down", [NE * HE, D]),
                    ("swgu", [D, 2 * HS]), ("sw_down", [HS, D])]:
        dr[nm] = nc.dram_tensor(nm, shp, F32, kind="ExternalInput")
    for nm, n in [("norm1_w", DT), ("norm2_w", DT), ("left_b", DT),
                  ("right_b", DT), ("out_b", DT), ("conv_b", DT),
                  ("down_b", DT), ("up1_b", FT), ("up2_b", FT),
                  ("ga_b", DT), ("gx_b", DT), ("a_param", DT),
                  ("conv_w", DT * 4), ("gate_w", DT * NE)]:
        dr[nm] = nc.dram_tensor(nm, [128, n], F32, kind="ExternalInput")
    dr["out"] = nc.dram_tensor("out", [128, DT * T], F32, kind="ExternalOutput")
    m.xsp = nc.dram_tensor("xsp", [128, DT * TMC], F32, kind="Internal")
    m.dr = dr

    m.init_constants()
    m.init_vectors(dr)
    m.vt0 = k.alloc("vt0", DT * 3, F32)
    m.vt1 = k.alloc("vt1", DT * 3, F32)
    m.hcar = k.alloc("hcar", DT, F32)
    k.v_memset(m.vt0[:, 0:DT * 3], 0.0, reads=(), writes=("vt0",))
    k.v_memset(m.hcar[:, 0:DT], 0.0, reads=(), writes=("hcar",))

    out3 = dr["out"].ap().rearrange("p (kt t) -> p kt t", kt=DT)
    xin3 = dr["xin"].ap().rearrange("p (kt t) -> p kt t", kt=DT)
    out_tokens = []
    n_mc = getattr(build_core, "N_MC", NMC)
    stages = getattr(build_core, "STAGES", ("ffn1", "attn", "ffn2"))

    def load_x(mc):
        xb = k.alloc(f"x_mc{mc}", DT * TMC, F32, high=True)
        xb3 = xb[:, 0:DT * TMC].rearrange("p (kt t) -> p kt t", kt=DT)
        tkeys = []
        for q in range(4):
            qk = f"{xb.name}.q{q}"
            k.dma("sync", xb3[:, 4 * q:4 * q + 4, :],
                  xin3[:, 4 * q:4 * q + 4, mc * TMC:(mc + 1) * TMC],
                  reads=(), writes=(xb.name, qk))
            tkeys.extend([qk] * 4)
        return xb, tkeys

    xb, tkeys = load_x(0)
    for mc in range(n_mc):
        xkey = xb.name
        if "ffn1" in stages:
            tkeys = m.ffn_stage(xb, xkey, mc, f"f1m{mc}", tile_keys=tkeys)
        if "attn" in stages:
            xb, tkeys = m.attn_stage(xb, xkey, mc, f"atm{mc}", tile_keys=tkeys)
            xkey = xb.name
        if "ffn2" in stages:
            tkeys = m.ffn_stage(xb, xkey, mc, f"f2m{mc}", tile_keys=tkeys)
        nxt = load_x(mc + 1) if mc + 1 < n_mc else None
        tok = k.dma("sync", out3[:, :, mc * TMC:(mc + 1) * TMC],
                    xb[:, 0:DT * TMC].rearrange("p (kt t) -> p kt t", kt=DT),
                    reads=(xkey,) + tuple(tkeys), writes=("outdram",))
        out_tokens.append(tok)
        for tk in tkeys:
            k.alias_key(xb.name, tk)
        k.free(xb.name)
        if nxt is not None:
            xb, tkeys = nxt
    k.barrier_wait("sync", out_tokens)
    k.build_and_emit(es)
    return nc, es, m



def _prep_vec(v):
    M = v.shape[0]
    return np.ascontiguousarray(v.reshape(M // 128, 128).T).astype(np.float32)


def _host_inputs(inputs):
    x = np.asarray(inputs["L0_attn_x"], dtype=np.float32)
    g = lambda n: np.asarray(inputs[n], dtype=np.float32)
    def merge_cols(wa, wb):
        R, M = wa.shape
        nt = M // 128
        out = np.concatenate([wa.reshape(R, nt, 128), wb.reshape(R, nt, 128)],
                             axis=2)
        return np.ascontiguousarray(out.reshape(R, 2 * M))

    wmats = {
        "rlW": merge_cols(g("L0_attn_right_W"), g("L0_attn_left_W")),
        "gagx": merge_cols(g("L0_attn_ga_W"), g("L0_attn_gx_W")),
        "out_W": g("L0_attn_out_W"),
        "up12": merge_cols(g("L0_attn_up1_W"), g("L0_attn_up2_W")),
        "down_W": g("L0_attn_down_W"),
        "wgu": merge_cols(g("L0_ffn_w_gate").reshape(NE * D, HE),
                          g("L0_ffn_w_up").reshape(NE * D, HE)),
        "w_down": np.ascontiguousarray(g("L0_ffn_w_down").reshape(NE * HE, D)),
        "swgu": merge_cols(g("L0_ffn_sw_gate"), g("L0_ffn_sw_up")),
        "sw_down": g("L0_ffn_sw_down"),
    }
    vecs = {
        "norm1_w": _prep_vec(g("L0_attn_norm1_w")),
        "norm2_w": _prep_vec(g("L0_attn_norm2_w")),
        "left_b": _prep_vec(g("L0_attn_left_b")),
        "right_b": _prep_vec(g("L0_attn_right_b")),
        "out_b": _prep_vec(g("L0_attn_out_b")),
        "conv_b": _prep_vec(g("L0_attn_conv_b")),
        "down_b": _prep_vec(g("L0_attn_down_b")),
        "up1_b": _prep_vec(g("L0_attn_up1_b")),
        "up2_b": _prep_vec(g("L0_attn_up2_b")),
        "ga_b": _prep_vec(g("L0_attn_ga_b")),
        "gx_b": _prep_vec(g("L0_attn_gx_b")),
        "a_param": _prep_vec(g("L0_attn_a_param")),
        "conv_w": np.ascontiguousarray(
            g("L0_attn_conv_w").reshape(4, DT, 128).transpose(2, 1, 0)
        ).reshape(128, DT * 4).astype(np.float32),
        "gate_w": np.ascontiguousarray(
            g("L0_ffn_gate_w").reshape(DT, 128, NE).transpose(1, 0, 2)
        ).reshape(128, DT * NE).astype(np.float32),
    }
    in_maps = []
    for b in range(NCORES):
        xb = np.ascontiguousarray(
            x[b].reshape(T, DT, 128).transpose(2, 1, 0)).reshape(128, DT * T)
        im = {"xin": xb}
        im.update(wmats)
        im.update(vecs)
        in_maps.append(im)
    return in_maps


def _host_output(res_list):
    outs = []
    for r in res_list:
        o = r["out"].reshape(128, DT, T).transpose(2, 1, 0).reshape(T, D)
        outs.append(o)
    return np.ascontiguousarray(np.stack(outs, axis=0)).astype(np.float32)


def kernel(**inputs):
    in_maps = _host_inputs(inputs)
    nc, es, m = build_core()
    try:
        res = run_bass_kernel_spmd(nc, in_maps, core_ids=list(range(NCORES)))
        return _host_output(res.results)
    finally:
        es.close()



# version 16
# speedup vs baseline: 1.1023x; 1.0795x over previous
import numpy as np
import concourse.bass as bass
import concourse.mybir as mybir
from concourse.bass_utils import run_bass_kernel_spmd

F32 = mybir.dt.float32
BF16 = mybir.dt.bfloat16
AF = mybir.ActivationFunctionType
ALU = mybir.AluOpType
AXX = mybir.AxisListType.X

NCORES = 8
D = 2048
T = 2048
TMC = 1024
NMC = T // TMC
DT = D // 128
NN = TMC // 512
HE = 1408
HT = HE // 128
NE = 8
HS = 2816
HST = HS // 128
FF = 6144
FT = FF // 128
FQ = 4
FQT = FT // FQ
EPS1 = 1e-5
EPS2 = 1e-8
CRG = 8.0

ENGS = ("sync", "scalar", "vector", "tensor", "gpsimd")

_DEBUG_TAPS = []

def set_debug_taps(taps):
    global _DEBUG_TAPS
    _DEBUG_TAPS = list(taps)



class Arena:

    def __init__(self, base, top):
        self.free_list = [(base, top - base)]
        self.live = {}
        self.dead = []

    def alloc(self, name, nbytes, align=32, high=False):
        nbytes = (nbytes + align - 1) & ~(align - 1)
        order = (range(len(self.free_list) - 1, -1, -1) if high
                 else range(len(self.free_list)))
        for i in order:
            off, size = self.free_list[i]
            if high:
                aoff = (off + size - nbytes) & ~(align - 1)
                if aoff < off:
                    continue
                newfree = []
                if aoff > off:
                    newfree.append((off, aoff - off))
                tail = (off + size) - (aoff + nbytes)
                if tail:
                    newfree.append((aoff + nbytes, tail))
            else:
                aoff = (off + align - 1) & ~(align - 1)
                pad = aoff - off
                if size - pad < nbytes:
                    continue
                rest = size - pad - nbytes
                newfree = []
                if pad:
                    newfree.append((off, pad))
                if rest:
                    newfree.append((aoff + nbytes, rest))
            self.free_list[i:i + 1] = newfree
            self.live[name] = (aoff, nbytes)
            hazards = []
            newdead = []
            for (o, s, n) in self.dead:
                if o + s <= aoff or o >= aoff + nbytes:
                    newdead.append((o, s, n))
                    continue
                hazards.append(n)
                if o < aoff:
                    newdead.append((o, aoff - o, n))
                if o + s > aoff + nbytes:
                    newdead.append((aoff + nbytes, o + s - (aoff + nbytes), n))
            self.dead = newdead
            return aoff, hazards
        raise MemoryError(
            f"SBUF arena: cannot alloc {name} ({nbytes}B); "
            f"free={self.free_list} live={ {k: v for k, v in self.live.items()} }")

    def free(self, name):
        off, size = self.live.pop(name)
        self.dead.append((off, size, name))
        self.free_list.append((off, size))
        self.free_list.sort()
        out = []
        for off, size in self.free_list:
            if out and out[-1][0] + out[-1][1] == off:
                out[-1] = (out[-1][0], out[-1][1] + size)
            else:
                out.append((off, size))
        self.free_list = out


class Buf:
    __slots__ = ("name", "h", "dtype", "nelem")

    def __init__(self, name, h, dtype, nelem):
        self.name = name
        self.h = h
        self.dtype = dtype
        self.nelem = nelem

    def ap(self):
        return self.h.ap()

    def __getitem__(self, sl):
        return self.h.ap()[sl]


class K:

    def __init__(self, nc):
        self.nc = nc
        self.ops = {e: [] for e in ENGS}
        self.cnt = {e: 0 for e in ENGS}
        self.dq = {"sync": 0, "gpsimd": 0, "scalar": 0}
        self.NDSLOT = {'gpsimd': 8, 'sync': 8, 'scalar': 8}
        self.dslot = {}
        self.last_w = {}
        self.readers = {}
        self.waited = {e: {} for e in ENGS}
        base = (nc.sbuf_base + 31) & ~31
        self.arena = Arena(base, nc.sbuf_top)
        self.bufs = {}
        self._uid = 0
        self.sems = {}

    def alloc(self, name, nelem, dtype, high=False):
        eb = 2 if dtype == BF16 else 4
        off, hazards = self.arena.alloc(name, nelem * eb, high=high)
        h = self.nc.alloc_sbuf_tensor_at(name + f"_{self._uid}", [128, nelem], dtype,
                                         offset=off)
        self._uid += 1
        b = Buf(name, h, dtype, nelem)
        self.bufs[name] = b
        toks = []
        for hn in hazards:
            if hn in self.last_w:
                toks.append(self.last_w[hn])
            toks.extend(self.readers.get(hn, ()))
        if toks:
            self.readers.setdefault(name, []).extend(toks)
        return b

    def free(self, name):
        pref = name + "."
        seen = set()
        for kk in list(self.last_w.keys()):
            if kk.startswith(pref):
                self.alias_key(name, kk)
                seen.add(kk)
        for kk in list(self.readers.keys()):
            if kk.startswith(pref) and kk not in seen:
                self.alias_key(name, kk)
        self.arena.free(name)
        del self.bufs[name]

    def alias_key(self, bufname, key):
        toks = list(self.readers.get(key, ()))
        t = self.last_w.get(key)
        if t:
            toks.append(t)
        self.readers.setdefault(bufname, []).extend(toks)

    def _dep_tokens(self, reads, writes):
        toks = []
        for k in reads:
            t = self.last_w.get(k)
            if t:
                toks.append(t)
        for k in writes:
            t = self.last_w.get(k)
            if t:
                toks.append(t)
            toks.extend(self.readers.get(k, ()))
        return toks

    def _commit(self, token, reads, writes):
        for k in writes:
            self.last_w[k] = token
            self.readers[k] = []
        for k in reads:
            self.readers.setdefault(k, []).append(token)

    def _emit_waits(self, engine, toks):
        need = {}
        for t in toks:
            kind, src, val = t
            if kind == "E" and src == engine:
                pass
            cur = need.get((kind, src), 0)
            if val > cur:
                need[(kind, src)] = val
        out = []
        for (kind, src), val in need.items():
            w = self.waited[engine]
            key = (kind, src)
            if w.get(key, 0) >= val:
                continue
            w[key] = val
            if kind == "E":
                out.append((src, val))
            else:
                out.append(("dma_" + src, 16 * val))
        return out

    def op(self, engine, fn, reads=(), writes=(), n_inc=1):
        toks = self._dep_tokens(reads, writes)
        waits = self._emit_waits(engine, toks)
        self.cnt[engine] += 1
        myval = self.cnt[engine]
        token = ("E", engine, myval)
        sem_of = self.sems

        def closure(eng, fn=fn, waits=waits, engine=engine, myval=myval):
            for sname, val in waits:
                eng.wait_ge(sem_of[sname], val)
            inst = fn(eng)
            inst.then_inc(sem_of[engine], 1)

        self.ops[engine].append(closure)
        self._commit(token, reads, writes)
        return token

    def dma(self, queue, out_ap, in_ap, reads=(), writes=(), **kw):
        toks = self._dep_tokens(reads, writes)
        slot = self.dq[queue] % self.NDSLOT[queue]
        self.dq[queue] += 1
        qs = f"{queue}:{slot}"
        prev = self.dslot.get(qs, 0)
        waits = self._emit_waits(queue, toks)
        self.dslot[qs] = prev + 1
        token = ("D", qs, self.dslot[qs])
        sem_of = self.sems

        def closure(eng, out_ap=out_ap, in_ap=in_ap, waits=waits, qs=qs, kw=kw):
            for sname, val in waits:
                eng.wait_ge(sem_of[sname], val)
            eng.dma_start(out=out_ap, in_=in_ap, **kw).then_inc(
                sem_of["dma_" + qs], 16)

        self.ops[queue].append(closure)
        self._commit(token, reads, writes)
        return token

    def full_barrier(self):
        toks = [("E", e, self.cnt[e]) for e in ENGS if self.cnt[e] > 0]
        toks += [("D", qs, n) for qs, n in self.dslot.items() if n > 0]
        for e in ENGS:
            self.barrier_wait(e, toks)

    def barrier_wait(self, engine, tokens):
        waits = self._emit_waits(engine, tokens)
        sem_of = self.sems

        def closure(eng, waits=waits):
            for sname, val in waits:
                eng.wait_ge(sem_of[sname], val)

        self.ops[engine].append(closure)

    def build_and_emit(self, es):
        nc = self.nc
        for e in ENGS:
            self.sems[e] = es.enter_context(nc.semaphore("sem_" + e))
        for q in self.dq:
            for s in range(self.NDSLOT[q]):
                self.sems[f"dma_{q}:{s}"] = es.enter_context(
                    nc.semaphore(f"dsem_{q}_{s}"))
        with nc.Block() as block:
            @block.sync
            def _(eng):
                for f in self.ops["sync"]:
                    f(eng)

            @block.scalar
            def _(eng):
                for f in self.ops["scalar"]:
                    f(eng)

            @block.vector
            def _(eng):
                for f in self.ops["vector"]:
                    f(eng)

            @block.tensor
            def _(eng):
                for f in self.ops["tensor"]:
                    f(eng)

            @block.gpsimd
            def _(eng):
                for f in self.ops["gpsimd"]:
                    f(eng)

    def v_tt(self, out, in0, in1, aluop, reads, writes):
        return self.op("vector", lambda e: e.tensor_tensor(out, in0, in1, aluop),
                       reads, writes)

    def v_stt(self, out, in0, scalar, in1, op0, op1, reads, writes):
        return self.op("vector",
                       lambda e: e.scalar_tensor_tensor(out, in0, scalar, in1, op0, op1),
                       reads, writes)

    def v_ts(self, out, in0, s1, s2, op0, op1, reads, writes):
        if s2 is None:
            return self.op("vector",
                           lambda e: e.tensor_scalar(out, in0, s1, None, op0),
                           reads, writes)
        return self.op("vector",
                       lambda e: e.tensor_scalar(out, in0, s1, s2, op0, op1),
                       reads, writes)

    def v_copy(self, out, in0, reads, writes):
        return self.op("vector", lambda e: e.tensor_copy(out, in0), reads, writes)

    def v_red(self, out, in0, aluop, reads, writes, negate=False):
        return self.op("vector",
                       lambda e: e.tensor_reduce(out, in0, AXX, aluop, negate=negate),
                       reads, writes)

    def v_recip(self, out, in0, reads, writes):
        return self.op("vector", lambda e: e.reciprocal(out, in0), reads, writes)

    def v_scan(self, out, a, b, init, reads, writes):
        return self.op("vector",
                       lambda e: e.tensor_tensor_scan(out, a, b, init, ALU.mult, ALU.add),
                       reads, writes)

    def v_memset(self, out, val, reads, writes):
        return self.op("vector", lambda e: e.memset(out, val), reads, writes)

    def s_act(self, out, in0, func, reads, writes, bias=0.0, scale=1.0):
        return self.op("scalar",
                       lambda e: e.activation(out, in0, func, bias=bias, scale=scale),
                       reads, writes)

    def g_tt(self, out, in0, in1, aluop, reads, writes):
        return self.op("gpsimd", lambda e: e.tensor_tensor(out, in0, in1, aluop),
                       reads, writes)

    def g_copy(self, out, in0, reads, writes):
        return self.op("gpsimd", lambda e: e.tensor_copy(out, in0), reads, writes)

    def mm_group(self, psum_ap, pairs, reads, writes, tag=None):
        if not hasattr(self, "mm_log"):
            self.mm_log = []
            self.mm_total = 0
        if tag is None:
            import traceback
            fr = traceback.extract_stack(limit=2)[0]
            tag = f"{fr.name}:{fr.lineno}"
        self.mm_total += len(pairs)
        self.mm_log.append((self.mm_total, tag))
        def fn(e):
            n = len(pairs)
            inst = None
            for i, (l, r) in enumerate(pairs):
                inst = e.matmul(psum_ap, l, r, start=(i == 0), stop=(i == n - 1))
            return inst
        return self.op("tensor", fn, reads, writes)

    def t_transpose(self, out_psum, in_sb, ident, reads, writes):
        if not hasattr(self, "mm_log"):
            self.mm_log = []
            self.mm_total = 0
        self.mm_total += 1
        self.mm_log.append((self.mm_total, "transpose"))
        return self.op("tensor", lambda e: e.transpose(out_psum, in_sb, ident),
                       reads, writes)



class Model:
    def __init__(self, nc, k: K):
        self.nc = nc
        self.k = k
        self.taps = []
        self.psi = 0

    def init_psum(self, es):
        self.pspool = []
        for i in range(4):
            h = es.enter_context(self.nc.psum_tensor(f"pp{i}", [128, 1024], F32))
            self.pspool.append((h, f"ps{i}"))

    def next_ps(self):
        h, key = self.pspool[self.psi % 4]
        self.psi += 1
        return h, key

    def tap(self, name, src_ap, nelem, dtype=F32):
        if name not in _DEBUG_TAPS:
            return
        h = self.nc.dram_tensor("tap_" + name, [128, nelem], dtype,
                                kind="ExternalOutput")
        keys = [f"tapkey_{name}"]
        self.taps.append(name)
        return h

    def tap_dma(self, name, src_ap, nelem, reads, dtype=F32):
        if name not in _DEBUG_TAPS:
            return
        h = self.nc.dram_tensor("tap_" + name, [128, nelem], dtype,
                                kind="ExternalOutput")
        self.k.dma("sync", h.ap(), src_ap, reads=reads, writes=())
        self.taps.append(name)

    def init_constants(self):
        k = self.k
        nc = self.nc
        self.ident = k.alloc("ident", 128, F32)
        self.oneh = k.alloc("oneh", NE * 128, F32)
        self.ones_col_bf = k.alloc("ones_col_bf", 1, BF16)
        self.ones_row = k.alloc("ones_row", 128, F32)

        k.op("gpsimd",
             lambda e: e.iota(self.ident[:, 0:128], pattern=[[-1, 128]], base=0,
                              channel_multiplier=1,
                              allow_small_or_imprecise_dtypes=True),
             reads=(), writes=("ident",))
        k.v_ts(self.ident[:, 0:128], self.ident[:, 0:128], 0.0, None,
               ALU.is_equal, None, reads=("ident",), writes=("ident",))
        k.op("gpsimd",
             lambda e: e.iota(self.oneh[0:NE, 0:NE * 128], pattern=[[-1, NE], [0, 128]],
                              base=0, channel_multiplier=1,
                              allow_small_or_imprecise_dtypes=True),
             reads=(), writes=("oneh",))
        k.v_ts(self.oneh[0:NE, 0:NE * 128], self.oneh[0:NE, 0:NE * 128], 0.0, None,
               ALU.is_equal, None, reads=("oneh",), writes=("oneh",))
        k.v_memset(self.ones_col_bf[:, 0:1], 1.0, reads=(), writes=("ones_col_bf",))
        k.v_memset(self.ones_row[0:1, 0:128], 1.0, reads=(), writes=("ones_row",))
        self.cb = k.alloc("cbias", 4, F32)
        for i, val in enumerate((EPS1, EPS2, 1.0)):
            k.v_memset(self.cb[:, i:i + 1], val, reads=(), writes=("cbias",))

    def cbias(self, idx):
        return self.cb[:, idx:idx + 1]

    def init_vectors(self, dr):
        k = self.k
        self.vec = {}
        specs = [
            ("norm1_w", DT), ("norm2_w", DT),
            ("left_b", DT), ("right_b", DT), ("out_b", DT), ("conv_b", DT),
            ("down_b", DT), ("up1_b", FT), ("up2_b", FT),
            ("ga_b", DT), ("gx_b", DT), ("a_param", DT),
            ("conv_w", DT * 4),
        ]
        for name, n in specs:
            b = k.alloc("v_" + name, n, F32)
            k.dma("sync", b[:, 0:n], dr[name].ap(), reads=(), writes=("v_" + name,))
            self.vec[name] = b
        gwf = k.alloc("v_gate_w_f", DT * NE, F32)
        k.dma("sync", gwf[:, 0:DT * NE], dr["gate_w"].ap(), reads=(),
              writes=("v_gate_w_f",))
        self.gwb = k.alloc("v_gate_w", DT * NE, BF16)
        k.v_copy(self.gwb[:, 0:DT * NE], gwf[:, 0:DT * NE],
                 reads=("v_gate_w_f",), writes=("v_gate_w",))
        for nm in ("ga_b", "gx_b"):
            b = k.alloc("v_" + nm + "_h", DT, F32)
            k.v_ts(b[:, 0:DT], self.vec[nm][:, 0:DT], 0.5, None, ALU.mult, None,
                   reads=("v_" + nm,), writes=("v_" + nm + "_h",))
            self.vec[nm + "_h"] = b
        ap_ = self.vec["a_param"]
        t1 = k.alloc("v_sp_t1", DT, F32)
        t2 = k.alloc("v_sp_t2", DT, F32)
        m4sp = k.alloc("v_m4sp", DT, F32)
        k.s_act(t1[:, 0:DT], ap_[:, 0:DT], AF.Abs, reads=("v_a_param",),
                writes=("v_sp_t1",))
        k.s_act(t1[:, 0:DT], t1[:, 0:DT], AF.Exp, reads=("v_sp_t1",),
                writes=("v_sp_t1",), scale=-1.0)
        k.s_act(t1[:, 0:DT], t1[:, 0:DT], AF.Ln, reads=("v_sp_t1", "cbias"),
                writes=("v_sp_t1",), bias=self.cb[:, 2:3])
        k.v_ts(t2[:, 0:DT], ap_[:, 0:DT], 0.0, None, ALU.max, None,
               reads=("v_a_param",), writes=("v_sp_t2",))
        k.v_tt(m4sp[:, 0:DT], t1[:, 0:DT], t2[:, 0:DT], ALU.add,
               reads=("v_sp_t1", "v_sp_t2"), writes=("v_m4sp",))
        k.v_ts(m4sp[:, 0:DT], m4sp[:, 0:DT], -4.0, None, ALU.mult, None,
               reads=("v_m4sp",), writes=("v_m4sp",))
        self.m4sp = m4sp

    def wblock_dma(self, dstbuf, dst_off, wdram, k0t, nk, m0, mb, wkey):
        src = wdram.ap().rearrange("(kt p) m -> p kt m", p=128)[
            :, k0t:k0t + nk, m0:m0 + mb]
        dst = dstbuf[:, dst_off:dst_off + nk * mb].rearrange(
            "p (kt m) -> p kt m", kt=nk)
        self.k.dma("gpsimd", dst, src, reads=(), writes=(wkey,))

    def stats_rows(self, xbuf, xkey, want_mean, tag, tile_keys=None):
        k = self.k
        xbf = k.alloc(f"st_xbf_{tag}", DT * TMC, BF16) if want_mean else None
        sq = k.alloc(f"st_sq_{tag}", DT * TMC, BF16)
        ssq_row = k.alloc(f"row_ssq_{tag}", TMC, F32)
        sum_row = k.alloc(f"row_sum_{tag}", TMC, F32) if want_mean else None
        onescol = self.ones_col_bf[:, 0:1]
        for kt in range(DT):
            sl = slice(kt * TMC, (kt + 1) * TMC)
            rk = (tile_keys[kt],) if tile_keys else (xkey,)
            if want_mean:
                k.v_copy(xbf[:, sl], xbuf[:, sl], reads=rk, writes=(xbf.name,))
                k.op("scalar",
                     lambda e, o=sq[:, sl], i=xbf[:, sl]:
                         e.activation(o, i, AF.Square),
                     reads=(xbf.name,), writes=(sq.name,))
            else:
                k.op("scalar",
                     lambda e, o=sq[:, sl], i=xbuf[:, sl]:
                         e.activation(o, i, AF.Square),
                     reads=rk, writes=(sq.name,))
        for n in range(NN):
            if want_mean:
                ps, pk = self.next_ps()
                pairs = [(onescol, xbf[:, kt * TMC + n * 512: kt * TMC + n * 512 + 512])
                         for kt in range(DT)]
                k.mm_group(ps.ap()[0:1, 0:512], pairs,
                           reads=(xbf.name, "ones_col_bf"), writes=(pk,))
                k.v_copy(sum_row[0:1, n * 512:(n + 1) * 512], ps.ap()[0:1, 0:512],
                         reads=(pk,), writes=(sum_row.name,))
            ps, pk = self.next_ps()
            pairs = [(onescol, sq[:, kt * TMC + n * 512: kt * TMC + n * 512 + 512])
                     for kt in range(DT)]
            k.mm_group(ps.ap()[0:1, 0:512], pairs, reads=(sq.name, "ones_col_bf"),
                       writes=(pk,))
            k.v_copy(ssq_row[0:1, n * 512:(n + 1) * 512], ps.ap()[0:1, 0:512],
                     reads=(pk,), writes=(ssq_row.name,))
        if xbf is not None:
            k.free(xbf.name)
        k.free(sq.name)
        return sum_row, ssq_row

    def rstd_row(self, out, outkey, var_ap, varkey, eps_idx, tag):
        k = self.k
        k.s_act(out, var_ap, AF.Ln, reads=(varkey, "cbias"), writes=(outkey,),
                bias=self.cb[0:1, eps_idx:eps_idx + 1])
        k.s_act(out, out, AF.Exp, reads=(outkey,), writes=(outkey,), scale=-0.5)

    def bcast_row(self, rowbuf, rowkey, tag):
        k = self.k
        B = k.alloc(f"bc_{tag}", TMC, F32)
        for n in range(NN):
            ps, pk = self.next_ps()
            k.mm_group(ps.ap()[:, 0:512],
                       [(self.ones_row[0:1, 0:128],
                         rowbuf[0:1, n * 512:(n + 1) * 512])],
                       reads=(rowkey, "ones_row"), writes=(pk,))
            k.v_copy(B[:, n * 512:(n + 1) * 512], ps.ap()[:, 0:512], reads=(pk,),
                     writes=(B.name,))
        return B

    def mm_stream(self, wdram, Mtiles, rhs_of, evac, tag, ring_slots=3,
                  rhs_keys=(), nk=DT, tpb=2):
        k = self.k
        nb = (Mtiles + tpb - 1) // tpb
        ring = []
        for i in range(ring_slots):
            bu = k.alloc(f"wr_{tag}_{i}", nk * 128 * tpb, BF16)
            ring.append(bu)

        def loadblk(bi):
            slot = ring[bi % ring_slots]
            m0 = bi * tpb * 128
            mb = min(tpb * 128, Mtiles * 128 - m0)
            self.wblock_dma(slot, 0, wdram, 0, nk, m0, mb, slot.name)

        for bi in range(min(2, nb)):
            loadblk(bi)
        for bi in range(nb):
            if bi + 2 < nb:
                loadblk(bi + 2)
            slot = ring[bi % ring_slots]
            mhere = min(tpb, Mtiles - bi * tpb)
            bw = mhere * 128
            for mi in range(mhere):
                mt = bi * tpb + mi
                ps, pk = self.next_ps()
                for n in range(NN):
                    pairs = [(slot[:, kt * bw + mi * 128: kt * bw + mi * 128 + 128],
                              rhs_of(kt, n)) for kt in range(nk)]
                    k.mm_group(ps.ap()[:, n * 512:(n + 1) * 512], pairs,
                               reads=(slot.name,) + tuple(rhs_keys), writes=(pk,))
                evac(mt, ps, pk)
        for bu in ring:
            k.free(bu.name)

    def ffn_stage(self, xb, xkey, mc, tag, tile_keys=None):
        k = self.k
        nc = self.nc
        sum_row, ssq_row = self.stats_rows(
            xb, xkey, True, tag,
            tile_keys=tile_keys if getattr(build_core, "TK_STATS", True) else None)
        m_row = k.alloc(f"row_m_{tag}", TMC, F32)
        v_row = k.alloc(f"row_v_{tag}", TMC, F32)
        s_row = k.alloc(f"row_s_{tag}", TMC, F32)
        rT = slice(0, TMC)
        k.v_ts(m_row[0:1, rT], sum_row[0:1, rT], 1.0 / D, None, ALU.mult, None,
               reads=(sum_row.name,), writes=(m_row.name,))
        k.op("vector", lambda e: e.tensor_tensor(v_row[0:1, rT], m_row[0:1, rT],
                                                 m_row[0:1, rT], ALU.mult),
             reads=(m_row.name,), writes=(v_row.name,))
        k.v_stt(v_row[0:1, rT], ssq_row[0:1, rT], 1.0 / D, v_row[0:1, rT],
                ALU.mult, ALU.subtract, reads=(ssq_row.name, v_row.name),
                writes=(v_row.name,))
        self.rstd_row(s_row[0:1, rT], s_row.name, v_row[0:1, rT], v_row.name,
                      0, tag)
        mB = self.bcast_row(m_row, m_row.name, f"m_{tag}")
        sB = self.bcast_row(s_row, s_row.name, f"s_{tag}")
        k.free(sum_row.name); k.free(ssq_row.name)
        k.free(m_row.name); k.free(v_row.name); k.free(s_row.name)
        u = k.alloc(f"u_{tag}", DT * TMC, BF16)
        tn = k.alloc(f"un_{tag}", TMC, F32)
        for kt in range(DT):
            sl = slice(kt * TMC, (kt + 1) * TMC)
            xrk = tile_keys[kt] if tile_keys else xkey
            k.v_tt(tn[:, 0:TMC], xb[:, sl], mB[:, 0:TMC], ALU.subtract,
                   reads=(xrk, mB.name), writes=(tn.name,))
            k.v_tt(u[:, sl], tn[:, 0:TMC], sB[:, 0:TMC], ALU.mult,
                   reads=(tn.name, sB.name), writes=(u.name,))
        k.free(tn.name)
        k.free(mB.name); k.free(sB.name)
        self.tap_dma(f"u_{tag}", u[:, 0:DT * TMC], DT * TMC, (u.name,), BF16)

        combT = k.alloc(f"combT_{tag}", TMC, F32)
        sm = k.alloc(f"sm_{tag}", 8, F32)
        sc = k.alloc(f"sc_{tag}", 8, F32)
        r1 = k.alloc(f"r1_{tag}", 8, F32)
        ct = k.alloc(f"ct_{tag}", 8, F32)
        red = k.alloc(f"red_{tag}", 4, F32)
        for tt in range(TMC // 128):
            ps, pk = self.next_ps()
            pairs = [(u[:, kt * TMC + tt * 128: kt * TMC + tt * 128 + 128],
                      self.gwb[:, kt * NE:(kt + 1) * NE]) for kt in range(DT)]
            k.mm_group(ps.ap()[:, 0:NE], pairs, reads=(u.name, "v_gate_w"),
                       writes=(pk,))
            k.v_red(red[:, 0:1], ps.ap()[:, 0:NE], ALU.max, reads=(pk,),
                    writes=(red.name,), negate=True)
            k.op("scalar", lambda e, o=sm[:, 0:NE], i=ps.ap()[:, 0:NE],
                 b=red[:, 0:1]: e.activation(o, i, AF.Exp, bias=b),
                 reads=(pk, red.name), writes=(sm.name,))
            k.v_red(red[:, 1:2], sm[:, 0:NE], ALU.add, reads=(sm.name,),
                    writes=(red.name,))
            k.v_recip(red[:, 2:3], red[:, 1:2], reads=(red.name,), writes=(red.name,))
            k.v_ts(sc[:, 0:NE], sm[:, 0:NE], red[:, 2:3], None, ALU.mult, None,
                   reads=(sm.name, red.name), writes=(sc.name,))
            k.v_red(red[:, 0:1], sc[:, 0:NE], ALU.max, reads=(sc.name,),
                    writes=(red.name,))
            k.v_ts(ct[:, 0:NE], sc[:, 0:NE], red[:, 0:1], None, ALU.is_ge, None,
                   reads=(sc.name, red.name), writes=(ct.name,))
            k.v_stt(ct[:, 0:NE], ct[:, 0:NE], -1e30, sc[:, 0:NE], ALU.mult, ALU.add,
                    reads=(ct.name, sc.name), writes=(ct.name,))
            k.v_red(red[:, 0:1], ct[:, 0:NE], ALU.max, reads=(ct.name,),
                    writes=(red.name,))
            k.v_ts(ct[:, 0:NE], sc[:, 0:NE], red[:, 0:1], None, ALU.is_ge, None,
                   reads=(sc.name, red.name), writes=(ct.name,))
            k.v_stt(ct[:, 0:NE], sc[:, 0:NE], 0.5, ct[:, 0:NE], ALU.mult, ALU.mult,
                    reads=(sc.name, ct.name), writes=(ct.name,))
            ps2, pk2 = self.next_ps()
            k.t_transpose(ps2.ap()[0:NE, 0:128], ct[:, 0:NE], self.ident[:, 0:128],
                          reads=(ct.name, "ident"), writes=(pk2,))
            k.v_copy(combT[0:NE, tt * 128:(tt + 1) * 128], ps2.ap()[0:NE, 0:128],
                     reads=(pk2,), writes=(combT.name,))
        for b in (sm, sc, r1, ct, red):
            k.free(b.name)
        self.tap_dma(f"combT_{tag}", combT[0:NE, 0:TMC], TMC, (combT.name,))

        combB = k.alloc(f"combB_{tag}", TMC, F32)
        gs = k.alloc(f"gs_{tag}", HT * TMC, BF16)
        silu_t = k.alloc(f"silu_{tag}", TMC, F32)
        wdq = [k.alloc(f"wd_{tag}_{i}", HT * 512, BF16) for i in range(2)]
        wgu_ring = [k.alloc(f"wgu_{tag}_{i}", DT * 256, BF16) for i in range(3)]
        dr = self.dr

        def load_wgu(e, ht):
            slot = wgu_ring[ht % 3]
            self.wblock_dma(slot, 0, dr["wgu"], e * DT, DT, ht * 256, 256,
                            slot.name)

        def load_wdq(e, q):
            buf = wdq[q % 2]
            srcv = dr["w_down"].ap().rearrange("(kt p) m -> p kt m", p=128)[
                :, e * HT:(e + 1) * HT, q * 512:(q + 1) * 512]
            dstv = buf[:, 0:HT * 512].rearrange("p (kt m) -> p kt m", kt=HT)
            k.dma("gpsimd", dstv, srcv, reads=(), writes=(buf.name,))

        for e in range(NE):
            for n in range(NN):
                ps, pk = self.next_ps()
                k.mm_group(ps.ap()[:, n * 512:(n + 1) * 512],
                           [(self.oneh[0:NE, e * 128:(e + 1) * 128],
                             combT[0:NE, n * 512:(n + 1) * 512])],
                           reads=(combT.name, "oneh"), writes=(pk,))
                k.s_act(combB[:, n * 512:(n + 1) * 512],
                        ps.ap()[:, n * 512:(n + 1) * 512], AF.Identity,
                        reads=(pk,), writes=(combB.name,))
            load_wgu(e, 0); load_wgu(e, 1)
            load_wdq(e, 0)
            for ht in range(HT):
                if ht + 2 < HT:
                    load_wgu(e, ht + 2)
                slot = wgu_ring[ht % 3]
                psg, pkg = self.next_ps()
                psu, pku = self.next_ps()
                for n in range(NN):
                    pairs = [(slot[:, kt * 256: kt * 256 + 128],
                              u[:, kt * TMC + n * 512: kt * TMC + n * 512 + 512])
                             for kt in range(DT)]
                    k.mm_group(psg.ap()[:, n * 512:(n + 1) * 512], pairs,
                               reads=(slot.name, u.name), writes=(pkg,))
                    pairs = [(slot[:, kt * 256 + 128: kt * 256 + 256],
                              u[:, kt * TMC + n * 512: kt * TMC + n * 512 + 512])
                             for kt in range(DT)]
                    k.mm_group(psu.ap()[:, n * 512:(n + 1) * 512], pairs,
                               reads=(slot.name, u.name), writes=(pku,))
                k.s_act(silu_t[:, 0:TMC], psg.ap()[:, 0:TMC], AF.Silu,
                        reads=(pkg,), writes=(silu_t.name,))
                k.v_tt(silu_t[:, 0:TMC], silu_t[:, 0:TMC], combB[:, 0:TMC],
                       ALU.mult, reads=(silu_t.name, combB.name),
                       writes=(silu_t.name,))
                k.v_tt(gs[:, ht * TMC:(ht + 1) * TMC], silu_t[:, 0:TMC],
                       psu.ap()[:, 0:TMC], ALU.mult,
                       reads=(silu_t.name, pku), writes=(gs.name,))
            for q in range(4):
                if q + 1 < 4:
                    load_wdq(e, q + 1)
                buf = wdq[q % 2]
                for mi in range(4):
                    mt = q * 4 + mi
                    ps, pk = self.next_ps()
                    for n in range(NN):
                        pairs = [(buf[:, kt * 512 + mi * 128: kt * 512 + mi * 128 + 128],
                                  gs[:, kt * TMC + n * 512: kt * TMC + n * 512 + 512])
                                 for kt in range(HT)]
                        k.mm_group(ps.ap()[:, n * 512:(n + 1) * 512], pairs,
                                   reads=(buf.name, gs.name), writes=(pk,))
                    sl = slice(mt * TMC, (mt + 1) * TMC)
                    xrk = (tile_keys[mt],) if (tile_keys and e == 0) else ()
                    k.v_tt(xb[:, sl], xb[:, sl], ps.ap()[:, 0:TMC], ALU.add,
                           reads=(xkey, pk) + xrk, writes=(xkey,))
        for b in wgu_ring:
            k.free(b.name)
        for b in wdq:
            k.free(b.name)
        k.free(gs.name); k.free(silu_t.name)
        k.free(combB.name); k.free(combT.name)

        sg_buf = k.alloc(f"sg_{tag}", HST * TMC, BF16)
        silu_t2 = k.alloc(f"silu2_{tag}", TMC, F32)
        swg_ring = [k.alloc(f"swg_{tag}_{i}", DT * 256, BF16) for i in range(3)]

        def load_swgu(mt):
            slot = swg_ring[mt % 3]
            self.wblock_dma(slot, 0, dr["swgu"], 0, DT, mt * 256, 256, slot.name)

        load_swgu(0); load_swgu(1)
        for mt in range(HST):
            if mt + 2 < HST:
                load_swgu(mt + 2)
            slot = swg_ring[mt % 3]
            psg, pkg = self.next_ps()
            psu, pku = self.next_ps()
            for n in range(NN):
                pairs = [(slot[:, kt * 256: kt * 256 + 128],
                          u[:, kt * TMC + n * 512: kt * TMC + n * 512 + 512])
                         for kt in range(DT)]
                k.mm_group(psg.ap()[:, n * 512:(n + 1) * 512], pairs,
                           reads=(slot.name, u.name), writes=(pkg,))
                pairs = [(slot[:, kt * 256 + 128: kt * 256 + 256],
                          u[:, kt * TMC + n * 512: kt * TMC + n * 512 + 512])
                         for kt in range(DT)]
                k.mm_group(psu.ap()[:, n * 512:(n + 1) * 512], pairs,
                           reads=(slot.name, u.name), writes=(pku,))
            k.s_act(silu_t2[:, 0:TMC], psg.ap()[:, 0:TMC], AF.Silu,
                    reads=(pkg,), writes=(silu_t2.name,))
            k.v_tt(sg_buf[:, mt * TMC:(mt + 1) * TMC], silu_t2[:, 0:TMC],
                   psu.ap()[:, 0:TMC], ALU.mult,
                   reads=(silu_t2.name, pku), writes=(sg_buf.name,))
        for b in swg_ring:
            k.free(b.name)
        k.free(silu_t2.name)
        swd2 = [k.alloc(f"swd_{tag}_{i}", HST * 256, BF16) for i in range(2)]

        def load_swd(dh):
            self.wblock_dma(swd2[dh % 2], 0, dr["sw_down"], 0, 11, dh * 256,
                            256, swd2[dh % 2].name)
            self.wblock_dma(swd2[dh % 2], 11 * 256, dr["sw_down"], 11, 11,
                            dh * 256, 256, swd2[dh % 2].name)

        load_swd(0)
        for dh in range(8):
            if dh + 1 < 8:
                load_swd(dh + 1)
            swd = swd2[dh % 2]
            for mt8 in range(2):
                mt = dh * 2 + mt8
                ps, pk = self.next_ps()
                for n in range(NN):
                    pairs = [(swd[:, kt * 256 + mt8 * 128: kt * 256 + mt8 * 128 + 128],
                              sg_buf[:, kt * TMC + n * 512: kt * TMC + n * 512 + 512])
                             for kt in range(HST)]
                    k.mm_group(ps.ap()[:, n * 512:(n + 1) * 512], pairs,
                               reads=(swd.name, sg_buf.name), writes=(pk,))
                sl = slice(mt * TMC, (mt + 1) * TMC)
                k.v_stt(xb[:, sl], ps.ap()[:, 0:TMC], 0.5, xb[:, sl],
                        ALU.mult, ALU.add, reads=(pk, xkey),
                        writes=(xkey, f"{xkey}.{tag}.t{mt}"))
        for bu in swd2:
            k.free(bu.name)
        k.free(sg_buf.name)
        k.free(u.name)
        return [f"{xkey}.{tag}.t{mt}" for mt in range(DT)]

    def attn_stage(self, xb, xkey, mc, tag, tile_keys=None):
        k = self.k
        dr = self.dr
        vec = self.vec
        rT = slice(0, TMC)
        sum_row, ssq_row = self.stats_rows(
            xb, xkey, True, tag,
            tile_keys=tile_keys if getattr(build_core, "TK_STATS", True) else None)
        m_row = k.alloc(f"row_m_{tag}", TMC, F32)
        v_row = k.alloc(f"row_v_{tag}", TMC, F32)
        s1_row = k.alloc(f"row_s1_{tag}", TMC, F32)
        sc_row = k.alloc(f"row_sc_{tag}", TMC, F32)
        t_row = k.alloc(f"row_t_{tag}", TMC, F32)
        k.v_ts(m_row[0:1, rT], sum_row[0:1, rT], 1.0 / D, None, ALU.mult, None,
               reads=(sum_row.name,), writes=(m_row.name,))
        k.v_tt(v_row[0:1, rT], m_row[0:1, rT], m_row[0:1, rT], ALU.mult,
               reads=(m_row.name,), writes=(v_row.name,))
        k.v_stt(v_row[0:1, rT], ssq_row[0:1, rT], 1.0 / D, v_row[0:1, rT],
                ALU.mult, ALU.subtract, reads=(ssq_row.name, v_row.name),
                writes=(v_row.name,))
        self.rstd_row(s1_row[0:1, rT], s1_row.name, v_row[0:1, rT], v_row.name,
                      0, tag)
        k.v_ts(t_row[0:1, rT], v_row[0:1, rT], EPS1, None, ALU.add, None,
               reads=(v_row.name,), writes=(t_row.name,))
        k.v_recip(t_row[0:1, rT], t_row[0:1, rT], reads=(t_row.name,),
                  writes=(t_row.name,))
        k.v_tt(t_row[0:1, rT], v_row[0:1, rT], t_row[0:1, rT], ALU.mult,
               reads=(v_row.name, t_row.name), writes=(t_row.name,))
        self.rstd_row(sc_row[0:1, rT], sc_row.name, t_row[0:1, rT], t_row.name,
                      1, tag)
        k.v_tt(sc_row[0:1, rT], sc_row[0:1, rT], s1_row[0:1, rT], ALU.mult,
               reads=(sc_row.name, s1_row.name), writes=(sc_row.name,))
        m1B = self.bcast_row(m_row, m_row.name, f"m1_{tag}")
        s1B = self.bcast_row(s1_row, s1_row.name, f"s1_{tag}")
        scB = self.bcast_row(sc_row, sc_row.name, f"sc_{tag}")
        for b in (sum_row, ssq_row, m_row, v_row, s1_row, sc_row, t_row):
            k.free(b.name)
        xn = k.alloc(f"xn_{tag}", DT * TMC, BF16)
        tn = k.alloc(f"tn_{tag}", TMC, F32)
        for kt in range(DT):
            sl = slice(kt * TMC, (kt + 1) * TMC)
            xrk = (tile_keys[kt] if (tile_keys and
                   getattr(build_core, "TK_XN", True)) else xkey)
            k.v_tt(tn[:, 0:TMC], xb[:, sl], m1B[:, 0:TMC], ALU.subtract,
                   reads=(xrk, m1B.name), writes=(tn.name,))
            k.v_stt(xn[:, sl], tn[:, 0:TMC], vec["norm1_w"][:, kt:kt + 1],
                    scB[:, 0:TMC], ALU.mult, ALU.mult,
                    reads=(tn.name, scB.name, "v_norm1_w"), writes=(xn.name,))
        k.free(tn.name)
        k.free(scB.name)
        self.tap_dma(f"xn_{tag}", xn[:, 0:DT * TMC], DT * TMC, (xn.name,), BF16)
        xsp3 = self.xsp.ap().rearrange("p (kt t) -> p kt t", kt=DT)
        xb3 = xb[:, 0:DT * TMC].rearrange("p (kt t) -> p kt t", kt=DT)
        k.dma("sync", xsp3, xb3,
              reads=(xkey,) + (tuple(tile_keys) if tile_keys else ()),
              writes=("xspill",))
        k.free(xb.name)

        vbuf = k.alloc(f"v_{tag}", DT * TMC, F32)
        left = k.alloc(f"left_{tag}", DT * TMC, BF16)
        cbf = k.alloc(f"cbf_{tag}", DT * TMC, BF16)
        ctmp = k.alloc(f"ctmp_{tag}", TMC, F32)
        cw = vec["conv_w"]
        cb = vec["conv_b"]
        halo = self.vt0 if mc == 0 else self.vt1
        halokey = "vt0" if mc == 0 else "vt1"
        rl_ring = [k.alloc(f"rl_{tag}_{i}", DT * 256, BF16) for i in range(3)]

        def load_rl(mt):
            slot = rl_ring[mt % 3]
            self.wblock_dma(slot, 0, dr["rlW"], 0, DT, mt * 256, 256, slot.name)

        def rhs_xn(kt, n):
            return xn[:, kt * TMC + n * 512: kt * TMC + n * 512 + 512]

        load_rl(0); load_rl(1)
        for mt in range(DT):
            if mt + 2 < DT:
                load_rl(mt + 2)
            slot = rl_ring[mt % 3]
            sl = slice(mt * TMC, (mt + 1) * TMC)
            psv, pkv = self.next_ps()
            psl, pkl = self.next_ps()
            for n in range(NN):
                pairs = [(slot[:, kt * 256: kt * 256 + 128], rhs_xn(kt, n))
                         for kt in range(DT)]
                k.mm_group(psv.ap()[:, n * 512:(n + 1) * 512], pairs,
                           reads=(slot.name, xn.name), writes=(pkv,))
                pairs = [(slot[:, kt * 256 + 128: kt * 256 + 256], rhs_xn(kt, n))
                         for kt in range(DT)]
                k.mm_group(psl.ap()[:, n * 512:(n + 1) * 512], pairs,
                           reads=(slot.name, xn.name), writes=(pkl,))
            k.s_act(vbuf[:, sl], psv.ap()[:, 0:TMC], AF.Identity,
                    reads=(pkv, "v_right_b"), writes=(vbuf.name,),
                    bias=vec["right_b"][:, mt:mt + 1])
            k.s_act(left[:, sl], psl.ap()[:, 0:TMC], AF.Gelu,
                    reads=(pkl, "v_left_b"), writes=(left.name,),
                    bias=vec["left_b"][:, mt:mt + 1])
            if mc == 0:
                k.v_copy(self.vt1[:, mt * 3:mt * 3 + 3],
                         vbuf[:, mt * TMC + TMC - 3: mt * TMC + TMC],
                         reads=(vbuf.name,), writes=("vt1",))
            k.v_ts(ctmp[:, 0:TMC], vbuf[:, sl], cw[:, mt * 4 + 3:mt * 4 + 4],
                   cb[:, mt:mt + 1], ALU.mult, ALU.add,
                   reads=(vbuf.name, "v_conv_w", "v_conv_b"), writes=(ctmp.name,))
            for s in range(1, 4):
                w_ap = cw[:, mt * 4 + 3 - s:mt * 4 + 4 - s]
                k.v_stt(ctmp[:, s:TMC], vbuf[:, mt * TMC: mt * TMC + TMC - s],
                        w_ap, ctmp[:, s:TMC], ALU.mult, ALU.add,
                        reads=(vbuf.name, "v_conv_w", ctmp.name),
                        writes=(ctmp.name,))
                k.v_stt(ctmp[:, 0:s], halo[:, mt * 3 + 3 - s: mt * 3 + 3],
                        w_ap, ctmp[:, 0:s], ALU.mult, ALU.add,
                        reads=(halokey, "v_conv_w", ctmp.name),
                        writes=(ctmp.name,))
            k.op("scalar",
                 lambda e, o=vbuf[:, sl], i=ctmp[:, 0:TMC]:
                     e.activation(o, i, AF.Copy),
                 reads=(ctmp.name,), writes=(vbuf.name,))
            k.op("scalar",
                 lambda e, o=cbf[:, sl], i=vbuf[:, sl]:
                     e.activation(o, i, AF.Copy),
                 reads=(vbuf.name,), writes=(cbf.name,))
        for bu in rl_ring:
            k.free(bu.name)
        k.free(ctmp.name)
        k.free(xn.name)
        cbuf = vbuf
        self.tap_dma(f"c_{tag}", cbuf[:, 0:DT * TMC], DT * TMC, (cbuf.name,))

        lr_all = cbuf.ap().bitcast(BF16)
        A = k.alloc(f"gA_{tag}", TMC, F32)
        Bv = k.alloc(f"gB_{tag}", TMC, F32)
        Cv = k.alloc(f"gC_{tag}", TMC, F32)
        Dv = k.alloc(f"gD_{tag}", TMC, F32)
        Ev = k.alloc(f"gE_{tag}", TMC, F32)
        gring = [k.alloc(f"gw_{tag}_{i}", DT * 256, BF16) for i in range(3)]

        def rhs_c(kt, n):
            return cbf[:, kt * TMC + n * 512: kt * TMC + n * 512 + 512]

        def load_gagx(mt):
            slot = gring[mt % 3]
            self.wblock_dma(slot, 0, dr["gagx"], 0, DT, mt * 256, 256, slot.name)

        load_gagx(0); load_gagx(1)
        for mt in range(DT):
            if mt + 2 < DT:
                load_gagx(mt + 2)
            slot = gring[mt % 3]
            psa, pka = self.next_ps()
            psb, pkb = self.next_ps()
            for n in range(NN):
                pairs = [(slot[:, kt * 256: kt * 256 + 128], rhs_c(kt, n))
                         for kt in range(DT)]
                k.mm_group(psa.ap()[:, n * 512:(n + 1) * 512], pairs,
                           reads=(slot.name, cbf.name), writes=(pka,))
                pairs = [(slot[:, kt * 256 + 128: kt * 256 + 256], rhs_c(kt, n))
                         for kt in range(DT)]
                k.mm_group(psb.ap()[:, n * 512:(n + 1) * 512], pairs,
                           reads=(slot.name, cbf.name), writes=(pkb,))
            k.s_act(A[:, 0:TMC], psa.ap()[:, 0:TMC], AF.Tanh,
                    reads=(pka, "v_ga_b_h"), writes=(A.name,),
                    bias=vec["ga_b_h"][:, mt:mt + 1], scale=0.5)
            k.s_act(Bv[:, 0:TMC], psb.ap()[:, 0:TMC], AF.Tanh,
                    reads=(pkb, "v_gx_b_h"), writes=(Bv.name,),
                    bias=vec["gx_b_h"][:, mt:mt + 1], scale=0.5)
            k.v_ts(A[:, 0:TMC], A[:, 0:TMC], self.m4sp[:, mt:mt + 1],
                   self.m4sp[:, mt:mt + 1], ALU.mult, ALU.add,
                   reads=(A.name, "v_m4sp"), writes=(A.name,))
            k.s_act(Cv[:, 0:TMC], A[:, 0:TMC], AF.Exp, reads=(A.name,),
                    writes=(Cv.name,))
            k.s_act(A[:, 0:TMC], Cv[:, 0:TMC], AF.Square, reads=(Cv.name,),
                    writes=(A.name,))
            k.v_ts(A[:, 0:TMC], A[:, 0:TMC], -1.0, 1.0, ALU.mult, ALU.add,
                   reads=(A.name,), writes=(A.name,))
            k.v_ts(A[:, 0:TMC], A[:, 0:TMC], 0.0, None, ALU.max, None,
                   reads=(A.name,), writes=(A.name,))
            k.s_act(A[:, 0:TMC], A[:, 0:TMC], AF.Ln, reads=(A.name,),
                    writes=(A.name,))
            k.s_act(A[:, 0:TMC], A[:, 0:TMC], AF.Exp, reads=(A.name,),
                    writes=(A.name,), scale=0.5)
            k.v_ts(Bv[:, 0:TMC], Bv[:, 0:TMC], 0.5, 0.5, ALU.mult, ALU.add,
                   reads=(Bv.name,), writes=(Bv.name,))
            k.v_tt(Dv[:, 0:TMC], Bv[:, 0:TMC],
                   cbuf[:, mt * TMC:(mt + 1) * TMC], ALU.mult,
                   reads=(Bv.name, cbuf.name), writes=(Dv.name,))
            k.v_tt(Dv[:, 0:TMC], A[:, 0:TMC], Dv[:, 0:TMC], ALU.mult,
                   reads=(A.name, Dv.name), writes=(Dv.name,))
            k.v_scan(Ev[:, 0:TMC], Cv[:, 0:TMC], Dv[:, 0:TMC],
                     self.hcar[:, mt:mt + 1], reads=(Cv.name, Dv.name, "hcar"),
                     writes=(Ev.name,))
            k.v_copy(self.hcar[:, mt:mt + 1], Ev[:, TMC - 1:TMC],
                     reads=(Ev.name,), writes=("hcar",))
            lr_mt = lr_all[:, mt * 2 * TMC: mt * 2 * TMC + TMC]
            k.v_tt(lr_mt, left[:, mt * TMC:(mt + 1) * TMC], Ev[:, 0:TMC],
                   ALU.mult, reads=(left.name, Ev.name), writes=("lr",))
            if mt == DT - 1:
                self.tap_dma(f"ga_{tag}", Cv[:, 0:TMC], TMC, (Cv.name,))
                self.tap_dma(f"gb_{tag}", Dv[:, 0:TMC], TMC, (Dv.name,))
                self.tap_dma(f"gh_{tag}", Ev[:, 0:TMC], TMC, (Ev.name,))
                self.tap_dma(f"gl_{tag}", left[:, mt * TMC:(mt + 1) * TMC], TMC,
                             (left.name,), BF16)
        for b in (A, Bv, Cv, Dv, Ev):
            k.free(b.name)
        for b in gring:
            k.free(b.name)
        k.free(cbf.name)
        k.free(left.name)

        if getattr(build_core, "DBG_BARRIER", "") == "preow":
            k.full_barrier()
        x1 = k.alloc(f"x1_{tag}", DT * TMC, F32)

        def rhs_lr(kt, n):
            return lr_all[:, kt * 2 * TMC + n * 512: kt * 2 * TMC + n * 512 + 512]

        def evac_o(mt, ps, pk):
            k.s_act(x1[:, mt * TMC:(mt + 1) * TMC], ps.ap()[:, 0:TMC],
                    AF.Identity, reads=(pk, "v_out_b"),
                    writes=(x1.name, f"{x1.name}.e{mt}"),
                    bias=vec["out_b"][:, mt:mt + 1])

        self.mm_stream(dr["out_W"], DT, rhs_lr, evac_o, f"ow_{tag}",
                       ring_slots=4, rhs_keys=("lr",))
        k.alias_key(cbuf.name, "lr")
        k.free(cbuf.name)
        if getattr(build_core, "DBG_BARRIER", "") == "postow":
            k.full_barrier()

        xres = k.alloc(f"xres_{tag}", DT * TMC, F32, high=True)
        xsp3 = self.xsp.ap().rearrange("p (kt t) -> p kt t", kt=DT)
        xres3 = xres[:, 0:DT * TMC].rearrange("p (kt t) -> p kt t", kt=DT)
        for q in range(4):
            k.dma("sync", xres3[:, 4 * q:4 * q + 4, :],
                  xsp3[:, 4 * q:4 * q + 4, :],
                  reads=("xspill",), writes=(xres.name, f"{xres.name}.q{q}"))
        self.tap_dma(f"x1pre_{tag}", x1[:, 0:DT * TMC], DT * TMC, (x1.name,))
        self.tap_dma(f"xres0_{tag}", xres[:, 0:DT * TMC], DT * TMC,
                     (xres.name, f"{xres.name}.q0", f"{xres.name}.q1",
                      f"{xres.name}.q2", f"{xres.name}.q3"))
        tn2 = k.alloc(f"tn2_{tag}", TMC, F32)
        tn2b = k.alloc(f"tn2b_{tag}", TMC, F32)
        x1tk = []
        xrtk = []
        for kt in range(DT):
            sl = slice(kt * TMC, (kt + 1) * TMC)
            qk = f"{xres.name}.q{kt // 4}"
            ek = f"{x1.name}.e{kt}"
            tkey = f"{x1.name}.t{kt}"
            rkey = f"{xres.name}.t{kt}"
            x1tk.append(tkey)
            xrtk.append(rkey)
            t_ = tn2
            opf = k.v_tt
            opf(t_[:, 0:TMC], xres[:, sl], m1B[:, 0:TMC], ALU.subtract,
                reads=(qk, m1B.name), writes=(t_.name,))
            opf(t_[:, 0:TMC], t_[:, 0:TMC], s1B[:, 0:TMC], ALU.mult,
                reads=(t_.name, s1B.name), writes=(t_.name,))
            opf(x1[:, sl], x1[:, sl], t_[:, 0:TMC], ALU.add,
                reads=(ek, t_.name), writes=(tkey,))
            opf(xres[:, sl], xres[:, sl], x1[:, sl], ALU.add,
                reads=(qk, tkey), writes=(rkey,))
        k.free(tn2.name)
        k.free(tn2b.name)
        k.free(m1B.name)
        k.free(s1B.name)
        self.tap_dma(f"x1_{tag}", x1[:, 0:DT * TMC], DT * TMC, (x1.name,))

        _, ssq2 = self.stats_rows(x1, x1.name, False, tag + "r2",
                                  tile_keys=x1tk)
        s2_row = k.alloc(f"row_s2_{tag}", TMC, F32)
        k.v_ts(s2_row[0:1, rT], ssq2[0:1, rT], 1.0 / D, None, ALU.mult, None,
               reads=(ssq2.name,), writes=(s2_row.name,))
        self.rstd_row(s2_row[0:1, rT], s2_row.name, s2_row[0:1, rT], s2_row.name,
                      1, tag)
        s2B = self.bcast_row(s2_row, s2_row.name, f"s2_{tag}")
        k.free(ssq2.name); k.free(s2_row.name)
        x1n = k.alloc(f"x1n_{tag}", DT * TMC, BF16)
        tn3 = k.alloc(f"tn3_{tag}", TMC, F32)
        for kt in range(DT):
            sl = slice(kt * TMC, (kt + 1) * TMC)
            k.v_tt(tn3[:, 0:TMC], x1[:, sl], s2B[:, 0:TMC], ALU.mult,
                   reads=(x1tk[kt], s2B.name), writes=(tn3.name,))
            k.v_ts(x1n[:, sl], tn3[:, 0:TMC], vec["norm2_w"][:, kt:kt + 1],
                   None, ALU.mult, None, reads=(tn3.name, "v_norm2_w"),
                   writes=(x1n.name,))
        k.free(tn3.name); k.free(s2B.name)
        for tk in x1tk:
            k.alias_key(x1.name, tk)
        k.free(x1.name)

        gq = k.alloc(f"gq_{tag}", FQT * TMC, BF16)
        dslab = k.alloc(f"dslab_{tag}", FQT * D, BF16)
        tgel = k.alloc(f"tgel_{tag}", TMC, F32)
        uring = [k.alloc(f"ur_{tag}_{i}", DT * 256, BF16) for i in range(3)]

        def rhs_x1n(kt, n):
            return x1n[:, kt * TMC + n * 512: kt * TMC + n * 512 + 512]

        def load_up12(ft):
            slot = uring[ft % 3]
            self.wblock_dma(slot, 0, dr["up12"], 0, DT, ft * 256, 256, slot.name)

        def load_dslab(fq):
            for mh in range(2):
                srcv = dr["down_W"].ap().rearrange("(kt p) m -> p kt m", p=128)[
                    :, fq * FQT:(fq + 1) * FQT, mh * 1024:(mh + 1) * 1024]
                dstv = dslab[:, 0:FQT * D].rearrange("p (kt m) -> p kt m",
                                                     kt=FQT)[
                    :, :, mh * 1024:(mh + 1) * 1024]
                k.dma("gpsimd", dstv, srcv, reads=(), writes=(dslab.name,))

        for fq in range(FQ):
            load_up12(fq * FQT); load_up12(fq * FQT + 1)
            load_dslab(fq)
            for ml in range(FQT):
                ft = fq * FQT + ml
                if ml + 2 < FQT:
                    load_up12(ft + 2)
                slot = uring[ft % 3]
                ps1, pk1 = self.next_ps()
                ps2, pk2 = self.next_ps()
                for n in range(NN):
                    pairs = [(slot[:, kt * 256: kt * 256 + 128], rhs_x1n(kt, n))
                             for kt in range(DT)]
                    k.mm_group(ps1.ap()[:, n * 512:(n + 1) * 512], pairs,
                               reads=(slot.name, x1n.name), writes=(pk1,))
                    pairs = [(slot[:, kt * 256 + 128: kt * 256 + 256],
                              rhs_x1n(kt, n)) for kt in range(DT)]
                    k.mm_group(ps2.ap()[:, n * 512:(n + 1) * 512], pairs,
                               reads=(slot.name, x1n.name), writes=(pk2,))
                k.s_act(tgel[:, 0:TMC], ps1.ap()[:, 0:TMC], AF.Gelu,
                        reads=(pk1, "v_up1_b"), writes=(tgel.name,),
                        bias=vec["up1_b"][:, ft:ft + 1])
                k.v_stt(gq[:, ml * TMC:(ml + 1) * TMC], ps2.ap()[:, 0:TMC],
                        vec["up2_b"][:, ft:ft + 1], tgel[:, 0:TMC],
                        ALU.add, ALU.mult,
                        reads=(pk2, "v_up2_b", tgel.name), writes=(gq.name,))
            for mt in range(DT):
                ps, pk = self.next_ps()
                for n in range(NN):
                    pairs = [(dslab[:, kt * D + mt * 128: kt * D + mt * 128 + 128],
                              gq[:, kt * TMC + n * 512: kt * TMC + n * 512 + 512])
                             for kt in range(FQT)]
                    k.mm_group(ps.ap()[:, n * 512:(n + 1) * 512], pairs,
                               reads=(dslab.name, gq.name), writes=(pk,))
                sl = slice(mt * TMC, (mt + 1) * TMC)
                rkey = f"{xres.name}.t{mt}"
                if fq == 0:
                    k.v_stt(xres[:, sl], ps.ap()[:, 0:TMC],
                            vec["down_b"][:, mt:mt + 1], xres[:, sl],
                            ALU.add, ALU.add,
                            reads=(pk, "v_down_b", rkey), writes=(rkey,))
                else:
                    k.v_tt(xres[:, sl], xres[:, sl], ps.ap()[:, 0:TMC], ALU.add,
                           reads=(rkey, pk), writes=(rkey,))
        for b in uring:
            k.free(b.name)
        k.free(gq.name); k.free(dslab.name); k.free(tgel.name)
        k.free(x1n.name)
        return xres, xrtk


def build_core():
    from contextlib import ExitStack
    nc = bass.Bass("TRN2", target_bir_lowering=False)
    es = ExitStack()
    k = K(nc)
    m = Model(nc, k)
    m.init_psum(es)

    dr = {}
    dr["xin"] = nc.dram_tensor("xin", [128, DT * T], F32, kind="ExternalInput")
    for nm, shp in [("rlW", [D, 2 * D]),
                    ("gagx", [D, 2 * D]), ("out_W", [D, D]),
                    ("up12", [D, 2 * FF]), ("down_W", [FF, D]),
                    ("wgu", [NE * D, 2 * HE]),
                    ("w_down", [NE * HE, D]),
                    ("swgu", [D, 2 * HS]), ("sw_down", [HS, D])]:
        dr[nm] = nc.dram_tensor(nm, shp, F32, kind="ExternalInput")
    for nm, n in [("norm1_w", DT), ("norm2_w", DT), ("left_b", DT),
                  ("right_b", DT), ("out_b", DT), ("conv_b", DT),
                  ("down_b", DT), ("up1_b", FT), ("up2_b", FT),
                  ("ga_b", DT), ("gx_b", DT), ("a_param", DT),
                  ("conv_w", DT * 4), ("gate_w", DT * NE)]:
        dr[nm] = nc.dram_tensor(nm, [128, n], F32, kind="ExternalInput")
    dr["out"] = nc.dram_tensor("out", [128, DT * T], F32, kind="ExternalOutput")
    m.xsp = nc.dram_tensor("xsp", [128, DT * TMC], F32, kind="Internal")
    m.dr = dr

    m.init_constants()
    m.init_vectors(dr)
    m.vt0 = k.alloc("vt0", DT * 3, F32)
    m.vt1 = k.alloc("vt1", DT * 3, F32)
    m.hcar = k.alloc("hcar", DT, F32)
    k.v_memset(m.vt0[:, 0:DT * 3], 0.0, reads=(), writes=("vt0",))
    k.v_memset(m.hcar[:, 0:DT], 0.0, reads=(), writes=("hcar",))

    out3 = dr["out"].ap().rearrange("p (kt t) -> p kt t", kt=DT)
    xin3 = dr["xin"].ap().rearrange("p (kt t) -> p kt t", kt=DT)
    out_tokens = []
    n_mc = getattr(build_core, "N_MC", NMC)
    stages = getattr(build_core, "STAGES", ("ffn1", "attn", "ffn2"))

    def load_x(mc):
        xb = k.alloc(f"x_mc{mc}", DT * TMC, F32, high=True)
        xb3 = xb[:, 0:DT * TMC].rearrange("p (kt t) -> p kt t", kt=DT)
        tkeys = []
        for q in range(4):
            qk = f"{xb.name}.q{q}"
            k.dma("sync", xb3[:, 4 * q:4 * q + 4, :],
                  xin3[:, 4 * q:4 * q + 4, mc * TMC:(mc + 1) * TMC],
                  reads=(), writes=(xb.name, qk))
            tkeys.extend([qk] * 4)
        return xb, tkeys

    xb, tkeys = load_x(0)
    for mc in range(n_mc):
        xkey = xb.name
        if "ffn1" in stages:
            tkeys = m.ffn_stage(xb, xkey, mc, f"f1m{mc}", tile_keys=tkeys)
        if "attn" in stages:
            xb, tkeys = m.attn_stage(xb, xkey, mc, f"atm{mc}", tile_keys=tkeys)
            xkey = xb.name
        if "ffn2" in stages:
            tkeys = m.ffn_stage(xb, xkey, mc, f"f2m{mc}", tile_keys=tkeys)
        nxt = load_x(mc + 1) if mc + 1 < n_mc else None
        tok = k.dma("sync", out3[:, :, mc * TMC:(mc + 1) * TMC],
                    xb[:, 0:DT * TMC].rearrange("p (kt t) -> p kt t", kt=DT),
                    reads=(xkey,) + tuple(tkeys), writes=("outdram",))
        out_tokens.append(tok)
        for tk in tkeys:
            k.alias_key(xb.name, tk)
        k.free(xb.name)
        if nxt is not None:
            xb, tkeys = nxt
    k.barrier_wait("sync", out_tokens)
    k.build_and_emit(es)
    return nc, es, m



def _prep_vec(v):
    M = v.shape[0]
    return np.ascontiguousarray(v.reshape(M // 128, 128).T).astype(np.float32)


def _host_inputs(inputs):
    x = np.asarray(inputs["L0_attn_x"], dtype=np.float32)
    g = lambda n: np.asarray(inputs[n], dtype=np.float32)
    def merge_cols(wa, wb):
        R, M = wa.shape
        nt = M // 128
        out = np.concatenate([wa.reshape(R, nt, 128), wb.reshape(R, nt, 128)],
                             axis=2)
        return np.ascontiguousarray(out.reshape(R, 2 * M))

    wmats = {
        "rlW": merge_cols(g("L0_attn_right_W"), g("L0_attn_left_W")),
        "gagx": merge_cols(g("L0_attn_ga_W"), g("L0_attn_gx_W")),
        "out_W": g("L0_attn_out_W"),
        "up12": merge_cols(g("L0_attn_up1_W"), g("L0_attn_up2_W")),
        "down_W": g("L0_attn_down_W"),
        "wgu": merge_cols(g("L0_ffn_w_gate").reshape(NE * D, HE),
                          g("L0_ffn_w_up").reshape(NE * D, HE)),
        "w_down": np.ascontiguousarray(g("L0_ffn_w_down").reshape(NE * HE, D)),
        "swgu": merge_cols(g("L0_ffn_sw_gate"), g("L0_ffn_sw_up")),
        "sw_down": g("L0_ffn_sw_down"),
    }
    vecs = {
        "norm1_w": _prep_vec(g("L0_attn_norm1_w")),
        "norm2_w": _prep_vec(g("L0_attn_norm2_w")),
        "left_b": _prep_vec(g("L0_attn_left_b")),
        "right_b": _prep_vec(g("L0_attn_right_b")),
        "out_b": _prep_vec(g("L0_attn_out_b")),
        "conv_b": _prep_vec(g("L0_attn_conv_b")),
        "down_b": _prep_vec(g("L0_attn_down_b")),
        "up1_b": _prep_vec(g("L0_attn_up1_b")),
        "up2_b": _prep_vec(g("L0_attn_up2_b")),
        "ga_b": _prep_vec(g("L0_attn_ga_b")),
        "gx_b": _prep_vec(g("L0_attn_gx_b")),
        "a_param": _prep_vec(g("L0_attn_a_param")),
        "conv_w": np.ascontiguousarray(
            g("L0_attn_conv_w").reshape(4, DT, 128).transpose(2, 1, 0)
        ).reshape(128, DT * 4).astype(np.float32),
        "gate_w": np.ascontiguousarray(
            g("L0_ffn_gate_w").reshape(DT, 128, NE).transpose(1, 0, 2)
        ).reshape(128, DT * NE).astype(np.float32),
    }
    in_maps = []
    for b in range(NCORES):
        xb = np.ascontiguousarray(
            x[b].reshape(T, DT, 128).transpose(2, 1, 0)).reshape(128, DT * T)
        im = {"xin": xb}
        im.update(wmats)
        im.update(vecs)
        in_maps.append(im)
    return in_maps


def _host_output(res_list):
    outs = []
    for r in res_list:
        o = r["out"].reshape(128, DT, T).transpose(2, 1, 0).reshape(T, D)
        outs.append(o)
    return np.ascontiguousarray(np.stack(outs, axis=0)).astype(np.float32)


def kernel(**inputs):
    in_maps = _host_inputs(inputs)
    nc, es, m = build_core()
    try:
        res = run_bass_kernel_spmd(nc, in_maps, core_ids=list(range(NCORES)))
        return _host_output(res.results)
    finally:
        es.close()

